# version 03
# baseline (speedup 1.0000x reference)
import math
import os
from contextlib import ExitStack

PH = int(os.environ.get("KPH", "9"))
CPH = int(os.environ.get("KCPH", "4"))
KCS = int(os.environ.get("KCS", "9"))
KCHH = int(os.environ.get("KCH", "8"))
KCG = int(os.environ.get("KCG", "2"))
KCQ = int(os.environ.get("KCQ", "3"))
SIMMODE = os.environ.get("KSIM", "") == "1"
IND_INC = 16

import numpy as np
import ml_dtypes

import concourse.bass as bass
import concourse.mybir as mybir
from concourse.bass import IndirectOffsetOnAxis
from concourse.bass_utils import run_bass_kernel_spmd

F32 = mybir.dt.float32
BF16 = mybir.dt.bfloat16
I32 = mybir.dt.int32
AF = mybir.ActivationFunctionType
ALU = mybir.AluOpType
AX = mybir.AxisListType

P = 128
D = 1024
S = 2048
NB = int(os.environ.get("KNB", "2"))
NCORES = 8
NT = S // P
KC = D // P
NH0 = 16
BF = ml_dtypes.bfloat16

ENGINES = ("sync", "gp", "pe", "act", "dve")


class Ctr:
    __slots__ = ("sem", "n")

    def __init__(self, sem):
        self.sem = sem
        self.n = 0


class Prog:
    def __init__(self, nc, es):
        self.nc = nc
        self.es = es
        self.q = {k: [] for k in ENGINES}
        self.waits = []
        self.events = {k: [] for k in ENGINES}
        self._semid = 0

    def ctr(self, name=None):
        name = name or f"s{self._semid}"
        self._semid += 1
        sem = self.es.enter_context(self.nc.semaphore(name))
        return Ctr(sem)

    def on(self, engine, fn):
        self.q[engine].append(fn)

    def wait(self, engine, ctr, val=None):
        v = ctr.n if val is None else val
        if v > 0:
            self.waits.append((engine, ctr, v))
            self.events[engine].append(("w", ctr, v))
            self.q[engine].append(lambda eng, s=ctr.sem, v=v: eng.wait_ge(s, v))

    def inc(self, instr, ctr, k=1):
        instr.then_inc(ctr.sem, k)
        return instr

    def op(self, engine, fn, ctr=None, k=1):
        if ctr is not None:
            ctr.n += k
            self.events[engine].append(("i", ctr, k))
        def wrapped(eng, fn=fn, ctr=ctr, k=k):
            instr = fn(eng)
            if ctr is not None:
                instr.then_inc(ctr.sem, k)
        self.q[engine].append(wrapped)

    def op_ind(self, engine, fn, ctr):
        ctr.n += IND_INC
        self.events[engine].append(("i", ctr, IND_INC))
        def wrapped(eng, fn=fn, ctr=ctr):
            instr = fn(eng)
            instr.then_inc(ctr.sem, 16)
        self.q[engine].append(wrapped)

    def check(self):
        bad = 0
        for engine, ctr, v in self.waits:
            if v > ctr.n:
                print(f"UNSATISFIABLE wait on {engine}: sem {ctr.sem.name} v={v} > final {ctr.n}")
                bad += 1
        if bad:
            raise AssertionError(f"{bad} unsatisfiable waits")
        pos = {e: 0 for e in ENGINES}
        cur = {}
        progress = True
        while progress:
            progress = False
            for e in ENGINES:
                while pos[e] < len(self.events[e]):
                    kind, ctr, v = self.events[e][pos[e]]
                    if kind == "i":
                        cur[id(ctr)] = cur.get(id(ctr), 0) + v
                        pos[e] += 1
                        progress = True
                    else:
                        if cur.get(id(ctr), 0) >= v:
                            pos[e] += 1
                            progress = True
                        else:
                            break
        stuck = [(e, pos[e], len(self.events[e])) for e in ENGINES
                 if pos[e] < len(self.events[e])]
        if stuck:
            for e, p_, tot in stuck:
                kind, ctr, v = self.events[e][p_]
                print(f"DEADLOCK: {e} stuck at event {p_}/{tot}: wait {ctr.sem.name} >= {v} (has {cur.get(id(ctr), 0)})")
            raise AssertionError("semaphore deadlock detected")

    def emit(self):
        nc = self.nc
        with nc.Block() as blk:
            @blk.sync
            def _(eng):
                for fn in self.q["sync"]:
                    fn(eng)

            @blk.gpsimd
            def _(eng):
                for fn in self.q["gp"]:
                    fn(eng)

            @blk.tensor
            def _(eng):
                for fn in self.q["pe"]:
                    fn(eng)

            @blk.scalar
            def _(eng):
                for fn in self.q["act"]:
                    fn(eng)

            @blk.vector
            def _(eng):
                for fn in self.q["dve"]:
                    fn(eng)


def make_consts():
    ident_bf = np.eye(P, dtype=np.float32).astype(BF)
    ident_f32 = np.eye(P, dtype=np.float32)
    tri_keep_bf = np.triu(np.ones((P, P), dtype=np.float32)).astype(BF)
    ones_row_bf = np.ones((1, P), dtype=BF)
    ones_row_f32 = np.ones((1, P), dtype=np.float32)
    cum_excl_bf = np.triu(np.ones((P, P), dtype=np.float32), 1).astype(BF)
    constpool = np.zeros((P, 8), np.float32)
    constpool[:, 0] = 1e-5
    return {
        "c_ident_bf": ident_bf, "c_ident_f32": ident_f32,
        "c_tri_keep_bf": tri_keep_bf, "c_ones_row_bf": ones_row_bf,
        "c_ones_row_f32": ones_row_f32, "c_cum_excl_bf": cum_excl_bf,
        "c_constpool": constpool,
    }


class Stage:
    def __init__(self):
        self.nc = bass.Bass("TRN2", target_bir_lowering=False)
        self.es = ExitStack()
        self.p = Prog(self.nc, self.es)

    def dram_in(self, name, shape, dtype=F32):
        return self.nc.dram_tensor(name, list(shape), dtype, kind="ExternalInput").ap()

    def dram_out(self, name, shape, dtype=F32):
        return self.nc.dram_tensor(name, list(shape), dtype, kind="ExternalOutput").ap()

    def dram_tmp(self, name, shape, dtype=F32):
        return self.nc.dram_tensor(name, list(shape), dtype, kind="Internal").ap()

    def sbuf(self, name, shape, dtype):
        return self.es.enter_context(self.nc.sbuf_tensor(name, list(shape), dtype))

    def psum(self, name, shape, dtype=F32):
        return self.es.enter_context(self.nc.psum_tensor(name, list(shape), dtype))

    def load_consts(self, ld):
        nc, p = self.nc, self.p
        cs = make_consts()
        self.cst = {}
        for k, arr in cs.items():
            it = nc.inline_tensor(arr, "il_" + k)
            dt = BF16 if arr.dtype == BF else F32
            t = self.sbuf(k, list(arr.shape), dt)
            p.op("gp", lambda eng, t=t, it=it: 
                eng.dma_start(out=t[:], in_=it.ap()), ld, 16)
            self.cst[k] = t
        nc.const_aps.register(F32, 1e-5, self.cst["c_constpool"][:, 0:1])

    def finish(self):
        self.p.check()
        self.p.emit()
        self.es.close()
        return self.nc


def tok_ap(dram, tg):
    return dram.rearrange("(n p) d -> n p d", p=P)[tg]



def build_stage_a():
    st = Stage()
    nc, p = st.nc, st.p

    h2 = st.dram_in("h2", [NB * S, D])
    wqvk = st.dram_in("Wqvk", [D, 3 * D])
    wo_d = st.dram_in("Wo", [D, D])
    rw = st.dram_in("router_w", [8, D])
    rb = st.dram_in("router_b", [8])

    xn_out = st.dram_out("xn_out", [NB * S, D], BF16)
    lg_out = st.dram_out("lg_out", [NB * S, 8], F32)
    x1p_out = st.dram_out("x1p_out", [NB * S, D], F32)
    dtmp16 = st.dram_tmp("dtmp16", [16], F32)
    if os.environ.get("KDBG"):
        dbg = {k: st.dram_out("dbg_" + k, [P, NT], F32) for k in ("sum", "ssq", "rstd", "nb", "mu", "var")}

    ld = p.ctr("ld")
    st.load_consts(ld)

    x0r = [st.sbuf(f"x0_{i}", [P, D], F32) for i in range(2)]
    xnr = [st.sbuf(f"xnr_{i}", [P, D], BF16) for i in range(2)]
    lnscr = [st.sbuf(f"lnscr{i}", [P, D], BF16) for i in range(2)]
    sum16 = st.sbuf("sum16", [P, NT], F32)
    ssq16 = st.sbuf("ssq16", [P, NT], F32)
    mu16 = st.sbuf("mu16", [P, NT], F32)
    rstd16 = st.sbuf("rstd16", [P, NT], F32)
    var16 = st.sbuf("var16", [P, NT], F32)
    nb16 = st.sbuf("nb16", [P, NT], F32)
    xnT = st.sbuf("xnT", [P, KC, S], BF16)
    qT = st.sbuf("qT", [P, KC, S], BF16)
    kT = st.sbuf("kT", [P, KC, S], BF16)
    vtok = st.sbuf("vtok", [P, NT, 8 * 65], BF16)
    attnNT = st.sbuf("attnNT", [P, KC, S], BF16)
    ptb = [st.sbuf(f"pt{i}", [P, 512], BF16) for i in range(3)]
    tmpU = [st.sbuf(f"tmpU{i}", [64, 512], BF16) for i in range(2)]
    wqb = st.sbuf("wqb", [P, KC, P], BF16)
    wkb = st.sbuf("wkb", [P, KC, P], BF16)
    wvb = st.sbuf("wvb", [P, KC, 512], BF16)
    wob = st.sbuf("wob", [P, KC, 512], BF16)
    rwb = st.sbuf("rwb", [8, D], BF16)
    rwTb = st.sbuf("rwTb", [P, KC, 8], BF16)
    rbcol = st.sbuf("rbcol", [8, 1], F32)
    lgT = st.sbuf("lgT", [8, S], F32)
    ltok = st.sbuf("ltok", [P, NT, 8], F32)
    qmx = st.sbuf("qmx", [P, 8], F32)
    kmx = st.sbuf("kmx", [P, 8], F32)
    s8q = st.sbuf("s8q", [8, P], F32)
    s8k = st.sbuf("s8k", [8, P], F32)
    qm8 = st.sbuf("qm8", [8, 2], F32)
    km8 = st.sbuf("km8", [8, 2], F32)
    nB82 = st.sbuf("nB82", [8, 2], F32)
    nBrow = st.sbuf("nBrow", [1, 16], F32)
    nBcol = st.sbuf("nBcol", [P, 1], F32)
    dnrow = st.sbuf("dnrow", [1, 512], F32)
    dnrb = st.sbuf("dnrb", [1, 512], BF16)

    psA = st.psum("psA", [P, 512])
    psB = st.psum("psB", [P, 512])
    po = [st.psum(f"po{i}", [65, 512]) for i in range(4)]
    pm1 = st.psum("pm1", [P, 512])
    pmb = st.psum("pmb", [P, 128], BF16)

    C = {k: p.ctr(k) for k in (
        "h", "st", "sq", "d1", "d2", "d3", "h2c", "xnc", "xst", "tp",
        "hA", "hB", "h2A", "h2B", "tpsA", "tpsB", "x0wA", "x0wB", "xsA", "xsB",
        "rw", "rwc", "rwt", "rwe",
        "lgm", "lge", "ltt", "lte", "lst",
        "wld", "wcv", "pm", "prj", "vun",
        "bm", "bt", "bte", "b8", "bnd",
        "neg", "nege", "sc", "exp", "rdy", "pvc", "oev", "dnr", "rbc", "nrm",
        "wol", "woc", "wm", "x0w", "xe", "xs")}

    def W(eng_name, cname, v):
        p.wait(eng_name, C[cname], v)

    if os.environ.get("KMEMSET"):
        p.op("dve", lambda eng: eng.memset(xnT[:], 0.0), C["vun"], 1)
    ld_full = ld.n
    for e in ENGINES:
        p.wait(e, ld, ld_full)

    p.op("gp", lambda eng: eng.dma_start(out=rwb[:], in_=rw), C["rw"], 16)
    p.op("gp", lambda eng: eng.dma_start(
        out=rbcol[:], in_=rb.rearrange("(e one) -> e one", one=1)), C["rw"], 16)
    W("pe", "rw", 32)
    for kcc in range(KC):
        p.op("pe", lambda eng, kcc=kcc: eng.transpose(
            out=pmb[0:P, kcc * 8:(kcc + 1) * 8],
            in_=rwb[0:8, kcc * P:(kcc + 1) * P],
            identity=st.cst["c_ident_bf"][0:8, 0:8]), C["rwt"], 1)
    W("dve", "rwt", KC)
    for kcc in range(KC):
        p.op("dve", lambda eng, kcc=kcc: eng.tensor_copy(
            out=rwTb[:, kcc, :], in_=pmb[0:P, kcc * 8:(kcc + 1) * 8]), C["rwe"], 1)

    for batch in range(NB):
        base = batch * NT
        snap = {k: C[k].n for k in C}

        if batch > 0:
            for e in ENGINES:
                for cn in ("pm", "lgm", "sc", "pvc", "wm", "xsA", "xsB", "lst",
                           "tpsA", "tpsB", "bm", "nrm", "lte"):
                    p.wait(e, C[cn], snap[cn])

        for t in range(NT):
            b = t % 2
            hn = "hA" if b == 0 else "hB"
            hS = C[hn]
            if t >= 2:
                W("gp", "st", snap["st"] + 2 * (t - 1))
                W("gp", "sq", snap["sq"] + (t - 1))
            p.op("gp", lambda eng, t=t, b=b, base=base:
                eng.dma_start(out=x0r[b][:], in_=tok_ap(h2, base + t)), hS, 16)
            W("act", hn, hS.n)
            if t >= 2:
                W("act", "st", snap["st"] + 2 * (t - 1))
            p.op("act", lambda eng, t=t, b=b:
                eng.activation(out=lnscr[b][:], in_=x0r[b][:], func=AF.Square), C["sq"], 1)
            W("dve", hn, hS.n)
            p.op("dve", lambda eng, t=t, b=b:
                eng.reduce_sum(out=sum16[:, t:t + 1], in_=x0r[b][:], axis=AX.X), C["st"], 1)
            W("dve", "sq", snap["sq"] + t + 1)
            p.op("dve", lambda eng, t=t, b=b:
                eng.reduce_sum(out=ssq16[:, t:t + 1], in_=lnscr[b][:], axis=AX.X), C["st"], 1)
        W("dve", "st", snap["st"] + 2 * NT)
        p.op("dve", lambda eng: eng.tensor_scalar_mul(
            out=mu16[:], in0=sum16[:], scalar1=1.0 / D), C["d1"], 1)
        W("dve", "d1", C["d1"].n)
        p.op("dve", lambda eng: eng.tensor_mul(
            out=nb16[:], in0=mu16[:], in1=mu16[:]), C["d1"], 1)
        p.op("dve", lambda eng: eng.tensor_scalar_mul(
            out=var16[:], in0=ssq16[:], scalar1=1.0 / D), C["d1"], 1)
        W("dve", "d1", C["d1"].n)
        p.op("dve", lambda eng: 
            eng.tensor_sub(out=var16[:], in0=var16[:], in1=nb16[:]), C["d1"], 1)
        W("act", "d1", C["d1"].n)
        p.op("act", lambda eng: eng.activation(out=rstd16[:], in_=var16[:], func=AF.Ln, bias=1e-5), C["d2"], 1)
        W("act", "d2", C["d2"].n)
        p.op("act", lambda eng:
            eng.activation(out=rstd16[:], in_=rstd16[:], func=AF.Exp, scale=-0.5), C["d2"], 1)
        W("dve", "d2", C["d2"].n)
        p.op("dve", lambda eng: eng.scalar_tensor_tensor(
            out=nb16[:], in0=mu16[:], scalar=-1.0, in1=rstd16[:],
            op0=ALU.mult, op1=ALU.mult), C["d3"], 1)
        W("dve", "d3", snap["d3"] + 1)
        for t in range(NT):
            b = t % 2
            h2n = "h2A" if b == 0 else "h2B"
            tpn = "tpsA" if b == 0 else "tpsB"
            h2S, tpS = C[h2n], C[tpn]
            if t < 2:
                W("gp", "st", snap["st"] + 2 * NT)
            else:
                W("gp", "xnc", snap["xnc"] + (t - 1))
            p.op("gp", lambda eng, t=t, b=b, base=base:
                eng.dma_start(out=x0r[b][:], in_=tok_ap(h2, base + t)), h2S, 16)
            W("dve", h2n, h2S.n)
            if t >= 2:
                W("dve", tpn, tpS.n)
            p.op("dve", lambda eng, t=t, b=b:
                eng.tensor_scalar(out=xnr[b][:], in0=x0r[b][:],
                                  scalar1=rstd16[:, t:t + 1], scalar2=nb16[:, t:t + 1],
                                  op0=ALU.mult, op1=ALU.add), C["xnc"], 1)
            W("sync", "xnc", snap["xnc"] + t + 1)
            p.op("sync", lambda eng, t=t, b=b, base=base:
                eng.dma_start(out=tok_ap(xn_out, base + t), in_=xnr[b][:]), tpS, 16)
            for k in range(KC):
                p.op("sync", lambda eng, t=t, b=b, k=k:
                    eng.dma_start(out=xnT[:, k, t * P:(t + 1) * P],
                                  in_=xnr[b][:, k * P:(k + 1) * P], transpose=True), tpS, 16)

        if PH < 2:
            continue
        W("pe", "rwe", KC)
        W("pe", "tpsA", C["tpsA"].n)
        W("pe", "tpsB", C["tpsB"].n)
        for blk4 in range(4):
            if blk4 > 0:
                W("pe", "lge", snap["lge"] + blk4)
            for kcc in range(KC):
                p.op("pe", lambda eng, blk4=blk4, kcc=kcc: eng.matmul(
                    out=pm1[0:8, 0:512], lhsT=rwTb[:, kcc, :],
                    rhs=xnT[:, kcc, blk4 * 512:(blk4 + 1) * 512],
                    start=(kcc == 0), stop=(kcc == KC - 1)), C["lgm"], 1)
            W("dve", "lgm", snap["lgm"] + (blk4 + 1) * KC)
            W("dve", "rw", 32)
            p.op("dve", lambda eng, blk4=blk4: eng.tensor_scalar_add(
                out=lgT[0:8, blk4 * 512:(blk4 + 1) * 512], in0=pm1[0:8, 0:512],
                scalar1=rbcol[:]), C["lge"], 1)
        KP2 = int(os.environ.get("KP2", "3"))
        if KP2 >= 2:
            W("pe", "lge", snap["lge"] + 4)
        for t in range(NT if KP2 >= 2 else 0):
            W("pe", "lge", snap["lge"] + (t // 4) + 1)
            if t > 0:
                W("pe", "lte", snap["lte"] + t)
            p.op("pe", lambda eng, t=t: eng.transpose(
                out=pm1[0:P, 0:8], in_=lgT[0:8, t * P:(t + 1) * P],
                identity=st.cst["c_ident_f32"][0:8, 0:8]), C["ltt"], 1)
            W("act", "ltt", snap["ltt"] + t + 1)
            p.op("act", lambda eng, t=t: eng.activation(
                out=ltok[:, t, :], in_=pm1[0:P, 0:8], func=AF.Copy), C["lte"], 1)
            if KP2 >= 3:
                W("gp", "lte", snap["lte"] + t + 1)
                p.op("gp", lambda eng, t=t, base_=base: eng.dma_start(
                    out=lg_out.rearrange("(n p) e -> n p e", p=P)[base_ + t],
                    in_=ltok[:, t, :]), C["lst"], 16)

        if PH < 3:
            continue
        tpA_full = C["tpsA"].n
        tpB_full = C["tpsB"].n
        for kf in range(KC):
            for side in ("q", "k"):
                wb_t = wqb if side == "q" else wkb
                coloff = 0 if side == "q" else 2 * D
                W("gp", "pm", max(0, C["pm"].n))
                p.op("gp", lambda eng, kf=kf, coloff=coloff, wb_t=wb_t: eng.dma_start(
                    out=wb_t[:],
                    in_=wqvk[:, coloff + kf * P: coloff + (kf + 1) * P]
                    .rearrange("(kc p) f -> p kc f", p=P)), C["wld"], 16)
                W("pe", "wld", C["wld"].n)
                W("pe", "tpsA", tpA_full)
                W("pe", "tpsB", tpB_full)
                dst = qT if side == "q" else kT
                for blk4 in range(4):
                    ps = psA if blk4 % 2 == 0 else psB
                    W("pe", "prj", max(0, C["prj"].n - 1))
                    for kcc in range(KC):
                        p.op("pe", lambda eng, ps=ps, kcc=kcc, blk4=blk4, wb_t=wb_t: 
                            eng.matmul(out=ps[:, 0:512], lhsT=wb_t[:, kcc, :],
                                       rhs=xnT[:, kcc, blk4 * 512:(blk4 + 1) * 512],
                                       start=(kcc == 0), stop=(kcc == KC - 1)), C["pm"], 1)
                    W("dve", "pm", C["pm"].n)
                    p.op("dve", lambda eng, ps=ps, dst=dst, kf=kf, blk4=blk4: 
                        eng.tensor_copy(out=dst[:, kf, blk4 * 512:(blk4 + 1) * 512],
                                        in_=ps[:, 0:512]), C["prj"], 1)
        if PH < 4:
            continue
        p.op("dve", lambda eng: eng.memset(
            vtok[:].rearrange("p t (h e) -> p t h e", e=65)[:, :, :, 64:65], 1.0), C["vun"], 1)

        prj_full = C["prj"].n
        W("dve", "prj", prj_full)
        for kf in range(KC):
            p.op("dve", lambda eng, kf=kf: eng.reduce_max(
                out=qmx[:, kf:kf + 1], in_=qT[:, kf, :], axis=AX.X,
                apply_absolute_value=True), C["bm"], 1)
            p.op("dve", lambda eng, kf=kf: eng.reduce_max(
                out=kmx[:, kf:kf + 1], in_=kT[:, kf, :], axis=AX.X,
                apply_absolute_value=True), C["bm"], 1)
        W("pe", "bm", snap["bm"] + 2 * KC)
        W("pe", "lte", snap["lte"] + NT)
        p.op("pe", lambda eng: eng.transpose(
            out=pm1[0:8, 0:P], in_=qmx[:], identity=st.cst["c_ident_f32"][:]), C["bt"], 1)
        W("dve", "bt", snap["bt"] + 1)
        p.op("dve", lambda eng: eng.tensor_copy(out=s8q[:], in_=pm1[0:8, 0:P]), C["bte"], 1)
        W("pe", "bte", snap["bte"] + 1)
        p.op("pe", lambda eng: eng.transpose(
            out=pm1[0:8, 0:P], in_=kmx[:], identity=st.cst["c_ident_f32"][:]), C["bt"], 1)
        W("dve", "bt", snap["bt"] + 2)
        p.op("dve", lambda eng: eng.tensor_copy(out=s8k[:], in_=pm1[0:8, 0:P]), C["bte"], 1)
        W("dve", "bte", C["bte"].n)
        p.op("dve", lambda eng: eng.reduce_max(
            out=qm8[:], in_=s8q[:].rearrange("c (j e) -> c j e", e=64), axis=AX.X), C["b8"], 1)
        p.op("dve", lambda eng: eng.reduce_max(
            out=km8[:], in_=s8k[:].rearrange("c (j e) -> c j e", e=64), axis=AX.X), C["b8"], 1)
        W("dve", "b8", C["b8"].n)
        p.op("dve", lambda eng: eng.tensor_mul(
            out=nB82[:], in0=qm8[:], in1=km8[:]), C["b8"], 1)
        W("dve", "b8", C["b8"].n)
        p.op("dve", lambda eng: eng.tensor_scalar(
            out=nB82[:], in0=nB82[:], scalar1=-2.0, scalar2=-60.0,
            op0=ALU.mult, op1=ALU.max), C["b8"], 1)
        W("sync", "b8", C["b8"].n)
        p.op("sync", lambda eng: eng.dma_start(
            out=dtmp16.rearrange("(c j) -> c j", j=2), in_=nB82[:]), C["bnd"], 16)
        W("sync", "bnd", snap["bnd"] + 16)
        p.op("sync", lambda eng: eng.dma_start(
            out=nBrow[:], in_=dtmp16.rearrange("(one f) -> one f", one=1)), C["bnd"], 16)

        trip_idx = 0
        hcount = batch * NH0
        for half in range(2):
            W("gp", "pm", C["pm"].n)
            p.op("gp", lambda eng, half=half: eng.dma_start(
                out=wvb[:], in_=wqvk[:, D + half * 512: D + (half + 1) * 512]
                .rearrange("(kc p) f -> p kc f", p=P)), C["wld"], 16)
            W("pe", "wld", C["wld"].n)
            vprj_base = C["prj"].n
            pvc_snapshot = C["pvc"].n
            for t in range(NT):
                ps = psA if t % 2 == 0 else psB
                W("pe", "prj", max(0, C["prj"].n - 1))
                if t < 2:
                    W("pe", "exp", C["exp"].n)
                for kcc in range(KC):
                    p.op("pe", lambda eng, ps=ps, t=t, kcc=kcc: eng.matmul(
                        out=ps[:, 0:512], lhsT=xnT[:, kcc, t * P:(t + 1) * P],
                        rhs=wvb[:, kcc, :],
                        start=(kcc == 0), stop=(kcc == KC - 1)), C["pm"], 1)
                W("dve", "pm", C["pm"].n)
                W("dve", "vun", snap["vun"] + 1)
                W("dve", "pvc", pvc_snapshot)
                p.op("dve", lambda eng, ps=ps, t=t: eng.tensor_copy(
                    out=vtok[:, t, :].rearrange("p (h e) -> p h e", e=65)[:, :, 0:64],
                    in_=ps[:, 0:512].rearrange("p (h e) -> p h e", e=64)), C["prj"], 1)
            vprj_full = C["prj"].n

            trip_base = trip_idx
            for hl in range(8):
                h = half * 8 + hl
                hk = h // 2
                r0 = 64 * (h % 2)
                W("pe", "bnd", snap["bnd"] + 32)
                W("pe", "nege", C["nege"].n)
                W("pe", "nrm", C["nrm"].n)
                p.op("pe", lambda eng, h=h: eng.matmul(
                    out=pm1[0:P, 0:1], lhsT=st.cst["c_ones_row_f32"][:],
                    rhs=nBrow[0:1, h:h + 1], start=True, stop=True), C["neg"], 1)
                W("act", "neg", C["neg"].n)
                W("act", "exp", C["exp"].n)
                p.op("act", lambda eng: eng.activation(
                    out=nBcol[:], in_=pm1[0:P, 0:1], func=AF.Copy), C["nege"], 1)
                W("act", "nege", C["nege"].n)
                for j in range(NT):
                    for k in range(j // 4, 4):
                        s0 = max(P * j, 512 * k)
                        w = 512 * (k + 1) - s0
                        i = trip_idx
                        ps = psA if i % 2 == 0 else psB
                        pt = ptb[i % 3]
                        if hl == 0 and i - trip_base < 2:
                            W("pe", "prj", vprj_full)
                        if i >= 2:
                            W("pe", "exp", snap["exp"] + i - 1)
                        p.op("pe", lambda eng, ps=ps, hk=hk, r0=r0, j=j, s0=s0, w=w: 
                            eng.matmul(out=ps[0:P, 0:w],
                                       lhsT=kT[r0:r0 + 64, hk, j * P:(j + 1) * P],
                                       rhs=qT[r0:r0 + 64, hk, s0:s0 + w],
                                       start=True, stop=True), C["sc"], 1)
                        W("act", "sc", C["sc"].n)
                        if i >= 3:
                            W("act", "pvc", snap["pvc"] + i - 2)
                        diag = (k == j // 4)
                        p.op("act", lambda eng, ps=ps, pt=pt, w=w: 
                            eng.activation(out=pt[0:P, 0:w], in_=ps[0:P, 0:w],
                                           func=AF.Exp, scale=1.0 / 32.0,
                                           bias=nBcol[:]), C["exp"], 1)
                        if diag:
                            W("dve", "exp", C["exp"].n)
                            p.op("dve", lambda eng, pt=pt: eng.tensor_mul(
                                out=pt[0:P, 0:P], in0=pt[0:P, 0:P],
                                in1=st.cst["c_tri_keep_bf"][:]), C["rdy"], 1)
                        W("pe", "exp", C["exp"].n)
                        if diag:
                            W("pe", "rdy", C["rdy"].n)
                        if j == 0:
                            W("pe", "oev", 8 * hcount)
                            if hl == 0:
                                W("pe", "prj", vprj_full)
                        c0 = s0 - 512 * k
                        p.op("pe", lambda eng, k=k, j=j, hl=hl, pt=pt, c0=c0, w=w: 
                            eng.matmul(out=po[k][0:65, c0:c0 + w],
                                       lhsT=vtok[:, j, 65 * hl:65 * hl + 65],
                                       rhs=pt[0:P, 0:w],
                                       start=(j == 0), stop=(j == 4 * k + 3)), C["pvc"], 1)
                        trip_idx += 1
                W("act", "pvc", C["pvc"].n)
                for k in range(4):
                    W("act", "dnr", max(0, C["dnr"].n))
                    p.op("act", lambda eng, k=k: eng.activation(
                        out=dnrow[:], in_=po[k][64:65, 0:512],
                        func=AF.Copy), C["oev"], 1)
                    W("dve", "oev", C["oev"].n)
                    if hcount > batch * NH0 or k > 0:
                        W("dve", "rbc", C["rbc"].n)
                    def _recip(eng):
                        with nc.allow_low_precision(reason="recip feeds bf16 bcast matmul"):
                            return eng.reciprocal(out=dnrb[:], in_=dnrow[:])
                    p.op("dve", _recip, C["dnr"], 1)
                    W("pe", "dnr", C["dnr"].n)
                    W("pe", "nrm", C["nrm"].n)
                    W("pe", "nege", C["nege"].n)
                    p.op("pe", lambda eng: eng.matmul(
                        out=pm1[0:64, 0:512], lhsT=st.cst["c_ones_row_bf"][0:1, 0:64],
                        rhs=dnrb[:], start=True, stop=True), C["rbc"], 1)
                    tu = tmpU[k % 2]
                    if k >= 2:
                        W("act", "nrm", C["nrm"].n - 1)
                    p.op("act", lambda eng, k=k, tu=tu: eng.activation(
                        out=tu[:], in_=po[k][0:64, 0:512], func=AF.Copy), C["oev"], 1)
                    W("dve", "rbc", C["rbc"].n)
                    W("dve", "oev", C["oev"].n)
                    p.op("dve", lambda eng, k=k, hk=hk, r0=r0, tu=tu: 
                        eng.tensor_mul(out=attnNT[r0:r0 + 64, hk, 512 * k:512 * (k + 1)],
                                       in0=tu[:], in1=pm1[0:64, 0:512]), C["nrm"], 1)
                hcount += 1

        if PH < 5:
            continue
        nrm_full = C["nrm"].n
        for half in range(2):
            if half > 0:
                W("gp", "wm", C["wm"].n)
            p.op("gp", lambda eng, half=half: eng.dma_start(
                out=wob[:], in_=wo_d[:, half * 512:(half + 1) * 512]
                .rearrange("(kc p) f -> p kc f", p=P)), C["wol"], 16)
            W("pe", "wol", C["wol"].n)
            W("pe", "nrm", nrm_full)
            for t in range(NT):
                ps = psA if t % 2 == 0 else psB
                xb = t % 2
                xsl = x0r[xb]
                x0S = C["x0wA"] if xb == 0 else C["x0wB"]
                xsS = C["xsA"] if xb == 0 else C["xsB"]
                W("pe", "xe", max(0, C["xe"].n - 1))
                if half == 0 and t < 2:
                    W("pe", "exp", C["exp"].n)
                for kcc in range(KC):
                    p.op("pe", lambda eng, ps=ps, t=t, kcc=kcc: eng.matmul(
                        out=ps[:, 0:512], lhsT=attnNT[:, kcc, t * P:(t + 1) * P],
                        rhs=wob[:, kcc, :],
                        start=(kcc == 0), stop=(kcc == KC - 1)), C["wm"], 1)
                if xsS.n > 0:
                    W("gp", ("xsA" if xb == 0 else "xsB"), xsS.n)
                p.op("gp", lambda eng, t=t, half=half, xsl=xsl, base=base: eng.dma_start(
                    out=xsl[:, 0:512],
                    in_=tok_ap(h2, base + t)[:, half * 512:(half + 1) * 512]), x0S, 16)
                W("dve", "wm", C["wm"].n)
                W("dve", ("x0wA" if xb == 0 else "x0wB"), x0S.n)
                p.op("dve", lambda eng, ps=ps, xsl=xsl: eng.tensor_add(
                    out=xsl[:, 0:512], in0=ps[:, 0:512], in1=xsl[:, 0:512]), C["xe"], 1)
                W("sync", "xe", C["xe"].n)
                p.op("sync", lambda eng, t=t, half=half, xsl=xsl, base=base: eng.dma_start(
                    out=tok_ap(x1p_out, base + t)[:, half * 512:(half + 1) * 512],
                    in_=xsl[:, 0:512]), xsS, 16)

    if os.environ.get("KDBG"):
        C_dbg = p.ctr("dbg")
        W("gp", "d3", C["d3"].n)
        for k, t in (("sum", sum16), ("ssq", ssq16), ("rstd", rstd16), ("nb", nb16), ("mu", mu16), ("var", var16)):
            p.op("gp", lambda eng, k=k, t=t: 
                eng.dma_start(out=dbg[k], in_=t[:]), C_dbg, 16)
        p.wait("gp", C_dbg, C_dbg.n)

    p.wait("sync", C["xsA"], C["xsA"].n)
    p.wait("sync", C["xsB"], C["xsB"].n)
    p.wait("sync", C["tpsA"], C["tpsA"].n)
    p.wait("sync", C["tpsB"], C["tpsB"].n)
    p.wait("gp", C["lst"], C["lst"].n)
    return st




NTOK = 16 * S
NTT = NTOK // P
CAP = 2048
GT = 64
OOB = float(2 ** 20)


def make_consts_b():
    tokid = (np.arange(NTT)[None, :] * P + np.arange(P)[:, None]).astype(np.float32)
    eoff = np.broadcast_to((np.arange(8) * CAP).astype(np.float32)[None, :], (P, 8)).copy()
    cum_excl_f32 = np.triu(np.ones((P, P), dtype=np.float32), 1)
    ones_col_f32 = np.ones((P, 1), dtype=np.float32)
    return {"c_tokid": tokid, "c_eoff": eoff, "c_cum_excl_f32": cum_excl_f32,
            "c_ones_col_f32": ones_col_f32}


def build_stage_b():
    st = Stage()
    nc, p = st.nc, st.p

    xn_full = st.dram_in("xn_full", [NTOK, D], BF16)
    lg_full = st.dram_in("lg_full", [P, NTT, 8], F32)
    we_d = st.dram_in("We", [D, 2 * D])
    ve_d = st.dram_in("Ve", [D, 2 * D])
    woe_d = st.dram_in("WOe", [2 * D, D])
    esel_d = st.dram_in("esel", [1, 8], F32)

    y_out = st.dram_out("y_out", [CAP, D], BF16)
    pos_out = st.dram_out("pos_out", [2, P, NTT], I32)

    tbl_d = st.dram_tmp("tbl", [CAP, 2], F32)
    off_d = st.dram_tmp("offd", [NTT, 8], F32)
    tot_d = st.dram_tmp("totd", [NTT, 8], F32)

    ld = p.ctr("ld")
    st.load_consts(ld)
    for k, arr in make_consts_b().items():
        it = nc.inline_tensor(arr, "il_" + k)
        t = st.sbuf(k, list(arr.shape), F32)
        p.op("gp", lambda eng, t=t, it=it:
            eng.dma_start(out=t[:], in_=it.ap()), ld, 16)
        st.cst[k] = t

    lg = st.sbuf("lg", [P, NTT, 8], F32)
    cC = None
    a0 = st.sbuf("a0", [P, NTT, 8], BF16)
    a1 = st.sbuf("a1", [P, NTT, 8], BF16)
    Ab = st.sbuf("Ab", [P, NTT, 8], BF16)
    aw = st.sbuf("aw", [P, NTT, 8], BF16)
    rk = st.sbuf("rk", [P, NTT, 8], F32)
    kp = st.sbuf("kp", [P, NTT, 8], BF16)
    v0 = st.sbuf("v0", [P, NTT], F32)
    v1 = st.sbuf("v1", [P, NTT], F32)
    w0 = st.sbuf("w0", [P, NTT], F32)
    w1c = st.sbuf("w1c", [P, NTT], F32)
    rkc = st.sbuf("rkc", [P, NTT], F32)
    kpc = st.sbuf("kpc", [P, NTT], F32)
    wc = st.sbuf("wc", [P, NTT], F32)
    idxf = st.sbuf("idxf", [P, NTT], F32)
    idx32 = st.sbuf("idx32", [P, NTT], I32)
    scb = st.sbuf("scb", [P, NTT * 2], F32)
    scb0 = st.sbuf("scb0", [P, 2], F32)
    eselb = st.sbuf("eselb", [P, 8], F32)
    esel_s = st.sbuf("esel_s", [1, 8], F32)
    totrow = st.sbuf("totrow", [1, NTT * 8], F32)
    offrow = st.sbuf("offrow", [1, NTT * 8], F32)
    t2 = [st.sbuf(f"t2_{i}", [P, 8], F32) for i in range(2)]
    trow = st.sbuf("trow", [1, 8], F32)
    gsc = st.sbuf("gsc", [P, GT, 8], F32)
    gsc2 = st.sbuf("gsc2", [P, GT, 8], F32)
    pos32 = st.sbuf("pos32", [P, GT], I32)
    tbl_s = st.sbuf("tbl_s", [P, 16, 2], F32)
    gidx = st.sbuf("gidx", [P, 16], I32)
    xg = [st.sbuf(f"xg{i}", [P, D], BF16) for i in range(2)]
    xgT = st.sbuf("xgT", [P, KC, CAP], BF16)
    web = st.sbuf("web", [P, KC, D], BF16)
    veb = st.sbuf("veb", [P, KC, D], BF16)
    wob2 = st.sbuf("wob2", [P, 16, D], BF16)
    silb = [st.sbuf(f"silb{i}", [P, 512], BF16) for i in range(2)]
    midg = st.sbuf("midg", [P, 16, 512], BF16)
    ytile = [st.sbuf(f"yt{i}", [P, D], BF16) for i in range(2)]

    psA2 = st.psum("psA2", [P, 512])
    psB2 = st.psum("psB2", [P, 512])
    psC2 = st.psum("psC2", [P, 512])
    psD2 = st.psum("psD2", [P, 512])
    pmr = st.psum("pmr", [P, 128])
    pms = st.psum("pms", [P, 128])
    pmt = st.psum("pmt", [1, 512])

    C = {k: p.ctr(k) for k in (
        "lgl", "rt", "pse", "pf", "pfe", "tot", "tote",
        "rko", "rke", "tre", "esm", "offe", "offl", "sel", "esl",
        "scat", "sct2", "tbi", "tbl", "gx0", "gx1", "gxh", "gxA", "gxB",
        "wl", "wlo", "mmv", "mmw", "mmo", "sil", "mid", "ye", "tpgA", "tpgB",
        "ystA", "ystB", "pos", "pcast", "poss")}

    def W(eng_name, cname, v):
        p.wait(eng_name, C[cname], v)

    ld_full = ld.n
    for e in ENGINES:
        p.wait(e, ld, ld_full)

    regs = {}

    def _mkregs(eng):
        r1 = eng.alloc_register("bcap")
        eng.reg_mov(r1, CAP - 1)
        r2 = eng.alloc_register("btok")
        eng.reg_mov(r2, NTOK - 1)
        regs["cap"] = r1
        regs["tok"] = r2
    p.op("gp", _mkregs)

    p.op("gp", lambda eng: eng.dma_start(out=lg[:], in_=lg_full), C["lgl"], 16)
    p.op("gp", lambda eng: eng.dma_start(out=esel_s[:], in_=esel_d), C["lgl"], 16)
    p.op("gp", lambda eng: eng.dma_start(
        out=wob2[:], in_=woe_d.rearrange("(kc p) f -> p kc f", p=P)), C["wlo"], 16)

    W("pe", "lgl", 32)
    p.op("pe", lambda eng: eng.matmul(
        out=pmr[0:P, 0:8], lhsT=st.cst["c_ones_row_f32"][:], rhs=esel_s[:],
        start=True, stop=True), C["esm"], 1)
    W("dve", "esm", C["esm"].n)
    p.op("dve", lambda eng: eng.tensor_copy(out=eselb[:], in_=pmr[0:P, 0:8]), C["esl"], 1)

    ngrp = NTT // GT
    W("dve", "lgl", 32)
    for g in range(ngrp):
        s = slice(g * GT, (g + 1) * GT)
        p.op("dve", lambda eng, s=s: eng.memset(rk[:, s, :], 0.0), C["rt"], 1)
        W("dve", "rt", C["rt"].n)
        for d in range(1, 8):
            p.op("dve", lambda eng, s=s, d=d: eng.tensor_tensor(
                out=gsc[:, :, 0:8 - d], in0=lg[:, s, 0:8 - d], in1=lg[:, s, d:8],
                op=ALU.is_ge), C["rt"], 1)
            p.op("dve", lambda eng, s=s, d=d: eng.tensor_tensor(
                out=gsc2[:, :, 0:8 - d], in0=lg[:, s, d:8], in1=lg[:, s, 0:8 - d],
                op=ALU.is_gt), C["rt"], 1)
            W("dve", "rt", C["rt"].n)
            p.op("dve", lambda eng, s=s, d=d: eng.tensor_tensor(
                out=rk[:, s, d:8], in0=rk[:, s, d:8], in1=gsc[:, :, 0:8 - d],
                op=ALU.add), C["rt"], 1)
            W("dve", "rt", C["rt"].n)
            p.op("dve", lambda eng, s=s, d=d: eng.tensor_tensor(
                out=rk[:, s, 0:8 - d], in0=rk[:, s, 0:8 - d], in1=gsc2[:, :, 0:8 - d],
                op=ALU.add), C["rt"], 1)
            W("dve", "rt", C["rt"].n)
        p.op("dve", lambda eng, s=s: eng.tensor_single_scalar(
            out=a0[:, s, :], in_=rk[:, s, :], scalar=0.0, op=ALU.is_equal), C["rt"], 1)
        p.op("dve", lambda eng, s=s: eng.tensor_single_scalar(
            out=a1[:, s, :], in_=rk[:, s, :], scalar=1.0, op=ALU.is_equal), C["rt"], 1)
        W("dve", "rt", C["rt"].n)
        p.op("dve", lambda eng, s=s: eng.tensor_tensor(
            out=Ab[:, s, :], in0=a0[:, s, :], in1=a1[:, s, :], op=ALU.add), C["rt"], 1)
        p.op("dve", lambda eng, s=s: eng.tensor_mul(
            out=gsc[:], in0=lg[:, s, :], in1=a0[:, s, :]), C["rt"], 1)
        W("dve", "rt", C["rt"].n)
        p.op("dve", lambda eng, s=s: eng.reduce_sum(
            out=v0[:, s], in_=gsc[:], axis=AX.X), C["rt"], 1)
        W("dve", "rt", C["rt"].n)
        p.op("dve", lambda eng, s=s: eng.tensor_mul(
            out=gsc[:], in0=lg[:, s, :], in1=a1[:, s, :]), C["rt"], 1)
        W("dve", "rt", C["rt"].n)
        p.op("dve", lambda eng, s=s: eng.reduce_sum(
            out=v1[:, s], in_=gsc[:], axis=AX.X), C["rt"], 1)
        W("dve", "rt", C["rt"].n)
        p.op("dve", lambda eng, s=s: eng.tensor_sub(
            out=w0[:, s], in0=v0[:, s], in1=v1[:, s]), C["rt"], 1)
    W("act", "rt", C["rt"].n)
    p.op("act", lambda eng: eng.activation(out=w0[:], in_=w0[:], func=AF.Sigmoid),
         C["pse"], 1)
    W("dve", "pse", C["pse"].n)
    p.op("dve", lambda eng: eng.tensor_scalar(
        out=w1c[:], in0=w0[:], scalar1=-1.0, scalar2=1.0, op0=ALU.mult, op1=ALU.add),
        C["rt"], 1)
    W("dve", "rt", C["rt"].n)
    for g in range(ngrp):
        s = slice(g * GT, (g + 1) * GT)
        W("dve", "rt", C["rt"].n)
        p.op("dve", lambda eng, s=s: eng.tensor_tensor(
            out=gsc[:], in0=a0[:, s, :],
            in1=w0[:, s].unsqueeze(2).broadcast_to([P, GT, 8]),
            op=ALU.mult), C["rt"], 1)
        p.op("dve", lambda eng, s=s: eng.tensor_tensor(
            out=gsc2[:], in0=a1[:, s, :],
            in1=w1c[:, s].unsqueeze(2).broadcast_to([P, GT, 8]),
            op=ALU.mult), C["rt"], 1)
        W("dve", "rt", C["rt"].n)
        p.op("dve", lambda eng, s=s: eng.tensor_tensor(
            out=aw[:, s, :], in0=gsc[:], in1=gsc2[:], op=ALU.add), C["rt"], 1)

    W("pe", "rt", C["rt"].n)
    for grp in range(NTT // 16):
        pm_ = pmr if grp % 2 == 0 else pms
        if grp >= 2:
            W("pe", "pfe", C["pfe"].n - 1)
        for tt in range(16):
            t = grp * 16 + tt
            p.op("pe", lambda eng, pm_=pm_, t=t, tt=tt: eng.matmul(
                out=pm_[0:P, tt * 8:(tt + 1) * 8], lhsT=st.cst["c_cum_excl_bf"][:],
                rhs=Ab[:, t, :], start=True, stop=True), C["pf"], 1)
        if grp >= 1:
            W("pe", "tote", C["tote"].n)
        for tt in range(16):
            t = grp * 16 + tt
            p.op("pe", lambda eng, t=t, tt=tt: eng.matmul(
                out=pmt[0:1, tt * 8:(tt + 1) * 8],
                lhsT=st.cst["c_tri_keep_bf"][:, 127:128],
                rhs=Ab[:, t, :], start=True, stop=True), C["tot"], 1)
        W("dve", "pf", C["pf"].n)
        p.op("dve", lambda eng, pm_=pm_, grp=grp: eng.tensor_copy(
            out=rk[:, grp * 16:(grp + 1) * 16, :].rearrange("p t e -> p (t e)"),
            in_=pm_[0:P, 0:128]), C["pfe"], 1)
        W("act", "tot", C["tot"].n)
        p.op("act", lambda eng, grp=grp: eng.activation(
            out=totrow[0:1, grp * 128:(grp + 1) * 128], in_=pmt[0:1, 0:128],
            func=AF.Copy), C["tote"], 1)

    W("sync", "tote", C["tote"].n)
    p.op("sync", lambda eng: eng.dma_start(
        out=tot_d.rearrange("(one t) e -> one (t e)", one=1), in_=totrow[:]), C["offl"], 16)
    W("sync", "offl", C["offl"].n)
    p.op("sync", lambda eng: eng.dma_start(
        out=t2[0][:], in_=tot_d.rearrange("(a p) e -> a p e", p=P)[0]), C["offl"], 16)
    p.op("sync", lambda eng: eng.dma_start(
        out=t2[1][:], in_=tot_d.rearrange("(a p) e -> a p e", p=P)[1]), C["offl"], 16)
    W("pe", "offl", C["offl"].n)
    W("pe", "pfe", C["pfe"].n)
    p.op("pe", lambda eng: eng.matmul(
        out=pmr[0:P, 0:8], lhsT=st.cst["c_cum_excl_f32"][:], rhs=t2[0][:],
        start=True, stop=True), C["rko"], 1)
    p.op("pe", lambda eng: eng.matmul(
        out=pmt[0:1, 0:8], lhsT=st.cst["c_ones_col_f32"][:], rhs=t2[0][:],
        start=True, stop=True), C["rko"], 1)
    W("act", "rko", C["rko"].n)
    p.op("act", lambda eng: eng.activation(out=trow[:], in_=pmt[0:1, 0:8], func=AF.Copy),
         C["tre"], 1)
    W("pe", "tre", C["tre"].n)
    p.op("pe", lambda eng: eng.matmul(
        out=pms[0:P, 0:8], lhsT=st.cst["c_cum_excl_f32"][:], rhs=t2[1][:],
        start=True, stop=False), C["rko"], 1)
    p.op("pe", lambda eng: eng.matmul(
        out=pms[0:P, 0:8], lhsT=st.cst["c_ones_row_f32"][:], rhs=trow[:],
        start=False, stop=True), C["rko"], 1)
    W("dve", "rko", C["rko"].n)
    p.op("dve", lambda eng: eng.tensor_copy(out=t2[0][:], in_=pmr[0:P, 0:8]), C["offe"], 1)
    p.op("dve", lambda eng: eng.tensor_copy(out=t2[1][:], in_=pms[0:P, 0:8]), C["offe"], 1)
    W("sync", "offe", C["offe"].n)
    p.op("sync", lambda eng: eng.dma_start(
        out=off_d.rearrange("(a p) e -> a p e", p=P)[0], in_=t2[0][:]), C["offl"], 16)
    p.op("sync", lambda eng: eng.dma_start(
        out=off_d.rearrange("(a p) e -> a p e", p=P)[1], in_=t2[1][:]), C["offl"], 16)
    W("sync", "offl", C["offl"].n)
    p.op("sync", lambda eng: eng.dma_start(
        out=offrow[:], in_=off_d.rearrange("(one t) e -> one (t e)", one=1)), C["offl"], 16)

    W("pe", "offl", C["offl"].n)
    W("dve", "esl", C["esl"].n)
    for grp in range(NTT // 16):
        pm_ = pmr if grp % 2 == 0 else pms
        if grp >= 2:
            W("pe", "rke", C["rke"].n - 1)
        p.op("pe", lambda eng, pm_=pm_, grp=grp: eng.matmul(
            out=pm_[0:P, 0:128], lhsT=st.cst["c_ones_row_f32"][:],
            rhs=offrow[0:1, grp * 128:(grp + 1) * 128], start=True, stop=True),
            C["rko"], 1)
        W("dve", "rko", C["rko"].n)
        p.op("dve", lambda eng, pm_=pm_, grp=grp: eng.tensor_tensor(
            out=rk[:, grp * 16:(grp + 1) * 16, :].rearrange("p t e -> p (t e)"),
            in0=rk[:, grp * 16:(grp + 1) * 16, :].rearrange("p t e -> p (t e)"),
            in1=pm_[0:P, 0:128], op=ALU.add), C["rke"], 1)

    W("dve", "rke", C["rke"].n)
    for g in range(ngrp):
        s = slice(g * GT, (g + 1) * GT)
        eb3 = eselb[:].unsqueeze(1).broadcast_to([P, GT, 8])
        eoff3 = st.cst["c_eoff"][:].unsqueeze(1).broadcast_to([P, GT, 8])
        W("dve", "pos", C["pos"].n)
        p.op("dve", lambda eng, s=s: eng.tensor_single_scalar(
            out=gsc[:], in_=rk[:, s, :], scalar=float(CAP), op=ALU.is_lt), C["sel"], 1)
        W("dve", "sel", C["sel"].n)
        p.op("dve", lambda eng, s=s: eng.tensor_tensor(
            out=kp[:, s, :], in0=gsc[:], in1=Ab[:, s, :], op=ALU.mult), C["sel"], 1)
        W("dve", "sel", C["sel"].n)
        p.op("dve", lambda eng, s=s, eb3=eb3: eng.tensor_tensor(
            out=gsc[:], in0=rk[:, s, :], in1=eb3, op=ALU.mult), C["sel"], 1)
        W("dve", "sel", C["sel"].n)
        p.op("dve", lambda eng, s=s: eng.reduce_sum(out=rkc[:, s], in_=gsc[:], axis=AX.X), C["sel"], 1)
        W("dve", "sel", C["sel"].n)
        p.op("dve", lambda eng, s=s, eb3=eb3: eng.tensor_tensor(
            out=gsc[:], in0=kp[:, s, :], in1=eb3, op=ALU.mult), C["sel"], 1)
        W("dve", "sel", C["sel"].n)
        p.op("dve", lambda eng, s=s: eng.reduce_sum(out=kpc[:, s], in_=gsc[:], axis=AX.X), C["sel"], 1)
        W("dve", "sel", C["sel"].n)
        p.op("dve", lambda eng, s=s, eb3=eb3: eng.tensor_tensor(
            out=gsc[:], in0=aw[:, s, :], in1=eb3, op=ALU.mult), C["sel"], 1)
        W("dve", "sel", C["sel"].n)
        p.op("dve", lambda eng, s=s: eng.reduce_sum(out=wc[:, s], in_=gsc[:], axis=AX.X), C["sel"], 1)
        for sl in (0, 1):
            asel = a0 if sl == 0 else a1
            W("dve", "pos", C["pos"].n)
            W("dve", "sel", C["sel"].n)
            p.op("dve", lambda eng, s=s, eoff3=eoff3: eng.tensor_tensor(
                out=gsc[:], in0=rk[:, s, :], in1=eoff3, op=ALU.add), C["pos"], 1)
            W("dve", "pos", C["pos"].n)
            p.op("dve", lambda eng, s=s, asel=asel: eng.tensor_tensor(
                out=gsc[:], in0=gsc[:], in1=asel[:, s, :], op=ALU.mult), C["pos"], 1)
            p.op("dve", lambda eng, s=s, asel=asel: eng.tensor_tensor(
                out=gsc2[:], in0=asel[:, s, :], in1=kp[:, s, :], op=ALU.mult), C["pos"], 1)
            W("dve", "pos", C["pos"].n)
            p.op("dve", lambda eng, s=s: eng.reduce_sum(out=v0[:, s], in_=gsc2[:], axis=AX.X), C["pos"], 1)
            p.op("dve", lambda eng, s=s: eng.reduce_sum(out=v1[:, s], in_=gsc[:], axis=AX.X), C["pos"], 1)
            W("dve", "pos", C["pos"].n)
            p.op("dve", lambda eng, s=s: eng.tensor_scalar(
                out=v0[:, s], in0=v0[:, s], scalar1=-OOB, scalar2=OOB,
                op0=ALU.mult, op1=ALU.add), C["pos"], 1)
            W("dve", "pos", C["pos"].n)
            p.op("dve", lambda eng, s=s: eng.tensor_add(
                out=v1[:, s], in0=v1[:, s], in1=v0[:, s]), C["pos"], 1)
            W("dve", "pos", C["pos"].n)
            if C["poss"].n > 0:
                W("dve", "poss", C["poss"].n)
            p.op("dve", lambda eng, s=s: eng.tensor_copy(
                out=pos32[:], in_=v1[:, s]), C["pcast"], 1)
            W("sync", "pcast", C["pcast"].n)
            p.op("sync", lambda eng, s=s, g=g, sl=sl: eng.dma_start(
                out=pos_out[sl][:, s], in_=pos32[:]), C["poss"], 16)

    W("dve", "sel", C["sel"].n)
    p.op("dve", lambda eng: eng.tensor_scalar(
        out=idxf[:], in0=kpc[:], scalar1=-OOB, scalar2=OOB, op0=ALU.mult, op1=ALU.add),
        C["scat"], 1)
    W("dve", "scat", C["scat"].n)
    p.op("dve", lambda eng: eng.tensor_add(out=idxf[:], in0=idxf[:], in1=rkc[:]), C["scat"], 1)
    W("dve", "scat", C["scat"].n)
    p.op("dve", lambda eng: eng.tensor_copy(out=idx32[:], in_=idxf[:]), C["scat"], 1)
    p.op("dve", lambda eng: eng.tensor_copy(
        out=scb[:].rearrange("p (t c) -> p t c", c=2)[:, :, 0:1],
        in_=st.cst["c_tokid"][:].unsqueeze(2)), C["scat"], 1)
    p.op("dve", lambda eng: eng.tensor_mul(
        out=scb[:].rearrange("p (t c) -> p t c", c=2)[:, :, 1:2],
        in0=wc[:].unsqueeze(2), in1=kpc[:].unsqueeze(2)), C["scat"], 1)
    p.op("dve", lambda eng: eng.memset(scb0[:, 0:1], OOB), C["tbi"], 1)
    p.op("dve", lambda eng: eng.memset(scb0[:, 1:2], 0.0), C["tbi"], 1)
    W("gp", "tbi", C["tbi"].n)
    for t in range(16):
        p.op("gp", lambda eng, t=t: eng.dma_start(
            out=tbl_d.rearrange("(t p) c -> t p c", p=P)[t], in_=scb0[:]), C["tbl"], 16)
    W("gp", "tbl", C["tbl"].n)
    W("gp", "scat", C["scat"].n)
    for t in range(NTT):
        if t > 0 and t % 16 == 0:
            W("gp", "sct2", C["sct2"].n)
        p.op("gp", lambda eng, t=t: eng.indirect_dma_start(
            out=tbl_d, out_offset=IndirectOffsetOnAxis(ap=idx32[:, t:t + 1], axis=0),
            in_=scb[:, 2 * t:2 * t + 2], in_offset=None,
            bounds_check=regs["cap"], oob_is_err=False), C["sct2"], 16)

    W("gp", "sct2", C["sct2"].n)
    p.op("gp", lambda eng: eng.dma_start(
        out=tbl_s[:], in_=tbl_d.rearrange("(t p) c -> p t c", p=P)), C["gxh"], 16)
    W("dve", "gxh", C["gxh"].n)
    p.op("dve", lambda eng: eng.tensor_copy(out=gidx[:], in_=tbl_s[:, :, 0]), C["gx1"], 1)
    W("gp", "gx1", C["gx1"].n)
    for t in range(16):
        b = t % 2
        tpS = C["tpgA"] if b == 0 else C["tpgB"]
        gS = C["gxA"] if b == 0 else C["gxB"]
        if t >= 2:
            W("gp", ("tpgA" if b == 0 else "tpgB"), tpS.n)
        p.op_ind("gp", lambda eng, t=t, b=b: eng.indirect_dma_start(
            out=xg[b][:], out_offset=None,
            in_=xn_full, in_offset=IndirectOffsetOnAxis(ap=gidx[:, t:t + 1], axis=0),
            bounds_check=regs["tok"], oob_is_err=False), gS)
        p.wait("sync", gS, gS.n)
        for k in range(KC):
            p.op("sync", lambda eng, t=t, b=b, k=k, tpS=tpS: eng.dma_start(
                out=xgT[:, k, t * P:(t + 1) * P], in_=xg[b][:, k * P:(k + 1) * P],
                transpose=True), tpS, 16)

    W("pe", "tpgA", C["tpgA"].n)
    W("pe", "tpgB", C["tpgB"].n)
    W("pe", "wlo", C["wlo"].n)
    for tb in range(4):
        for hb in range(2):
            if tb > 0 or hb > 0:
                W("gp", "mmv", C["mmv"].n)
                W("gp", "mmw", C["mmw"].n)
            p.op("gp", lambda eng, hb=hb: eng.dma_start(
                out=veb[:], in_=ve_d[:, hb * D:(hb + 1) * D]
                .rearrange("(kc p) f -> p kc f", p=P)), C["wl"], 16)
            p.op("gp", lambda eng, hb=hb: eng.dma_start(
                out=web[:], in_=we_d[:, hb * D:(hb + 1) * D]
                .rearrange("(kc p) f -> p kc f", p=P)), C["wl"], 16)
            W("pe", "wl", C["wl"].n)
            for fl in range(8):
                fc = hb * 8 + fl
                psV = psA2 if fc % 2 == 0 else psB2
                psW = psC2 if fc % 2 == 0 else psD2
                sb_ = silb[fc % 2]
                if fc >= 2 or tb > 0:
                    W("pe", "sil", max(0, C["sil"].n - 1))
                for kcc in range(KC):
                    p.op("pe", lambda eng, psV=psV, fl=fl, tb=tb, kcc=kcc: eng.matmul(
                        out=psV[:, 0:512], lhsT=veb[:, kcc, fl * P:(fl + 1) * P],
                        rhs=xgT[:, kcc, tb * 512:(tb + 1) * 512],
                        start=(kcc == 0), stop=(kcc == KC - 1)), C["mmv"], 1)
                W("act", "mmv", C["mmv"].n)
                if fc >= 2 or tb > 0:
                    W("act", "mid", max(0, C["mid"].n - 1))
                p.op("act", lambda eng, psV=psV, sb_=sb_: eng.activation(
                    out=sb_[:], in_=psV[:, 0:512], func=AF.Silu), C["sil"], 1)
                if fc >= 2 or tb > 0:
                    W("pe", "mid", max(0, C["mid"].n - 1))
                for kcc in range(KC):
                    p.op("pe", lambda eng, psW=psW, fl=fl, tb=tb, kcc=kcc: eng.matmul(
                        out=psW[:, 0:512], lhsT=web[:, kcc, fl * P:(fl + 1) * P],
                        rhs=xgT[:, kcc, tb * 512:(tb + 1) * 512],
                        start=(kcc == 0), stop=(kcc == KC - 1)), C["mmw"], 1)
                W("dve", "mmw", C["mmw"].n)
                W("dve", "sil", C["sil"].n)
                p.op("dve", lambda eng, psW=psW, fc=fc, sb_=sb_: eng.tensor_mul(
                    out=midg[:, fc, :], in0=psW[:, 0:512], in1=sb_[:]),
                    C["mid"], 1)
        mid_full = C["mid"].n
        W("pe", "mid", mid_full)
        W("dve", "gx0", C["gx0"].n)
        for tt in range(4):
            tok = tb * 4 + tt
            yb = ytile[tt % 2]
            ystS = C["ystA"] if tt % 2 == 0 else C["ystB"]
            ystn = "ystA" if tt % 2 == 0 else "ystB"
            for half in range(2):
                pso = psA2 if half == 0 else psB2
                W("pe", "ye", max(0, C["ye"].n - 1))
                if tb == 0 and tt == 0:
                    W("pe", "sil", C["sil"].n)
                for fc in range(16):
                    p.op("pe", lambda eng, pso=pso, tt=tt, half=half, fc=fc: eng.matmul(
                        out=pso[:, 0:512],
                        lhsT=midg[:, fc, tt * P:(tt + 1) * P],
                        rhs=wob2[:, fc, half * 512:(half + 1) * 512],
                        start=(fc == 0), stop=(fc == 15)), C["mmo"], 1)
                W("dve", "mmo", C["mmo"].n)
                if ystS.n > 0:
                    W("dve", ystn, ystS.n)
                p.op("dve", lambda eng, pso=pso, yb=yb, tok=tok, half=half:
                    eng.tensor_scalar_mul(
                        out=yb[:, half * 512:(half + 1) * 512], in0=pso[:, 0:512],
                        scalar1=tbl_s[:, tok, 1:2]), C["ye"], 1)
            W("gp", "ye", C["ye"].n)
            p.op("gp", lambda eng, tok=tok, yb=yb: eng.dma_start(
                out=y_out.rearrange("(t p) d -> t p d", p=P)[tok], in_=yb[:]), ystS, 16)
    p.wait("gp", C["ystA"], C["ystA"].n)
    p.wait("gp", C["ystB"], C["ystB"].n)
    p.wait("sync", C["poss"], C["poss"].n)
    return st


NHH = 8
HD = 128
FQ = 1024
NFQ = 4


def gammas_np():
    lo, hi = math.log(1.0 / 512.0), math.log(1.0 / 32.0)
    return 1.0 - np.exp(np.linspace(lo, hi, NHH))


def make_consts_c():
    g = gammas_np()
    pcol = np.arange(P, dtype=np.float64)
    gq = np.zeros((1, NHH * 512), np.float64)
    for h in range(NHH):
        gq[0, h * 512:(h + 1) * 512] = g[h] ** np.arange(512)
    scol = np.zeros((P, NHH * 19), np.float64)
    for h in range(NHH):
        for dt in range(-3, 16):
            scol[:, h * 19 + dt + 3] = np.exp(math.log(g[h]) * (128.0 * dt - pcol))
    return {"c_gq": gq.astype(np.float32).astype(BF),
            "c_scol": scol.astype(np.float32)}


def build_stage_c():
    st = Stage()
    nc, p = st.nc, st.p

    x1p_d = st.dram_in("x1p", [NB * S, D], F32)
    pos_d = st.dram_in("pos", [2, P, NB * NT], I32)
    yfull_d = st.dram_in("yfull", [8 * CAP, D], BF16)
    wq_d = st.dram_in("Wq", [NHH, HD, HD], F32)
    wk_d = st.dram_in("Wk", [NHH, HD, HD], F32)
    wv_d = st.dram_in("Wv", [NHH, HD, HD], F32)
    w1r_d = st.dram_in("W1r", [D, D], F32)
    w2r_d = st.dram_in("W2r", [D, D], F32)
    gnw_d = st.dram_in("gnw", [1, D], F32)
    gnb_d = st.dram_in("gnb", [1, D], F32)
    wr_d = st.dram_in("Wr", [D, 8], F32)
    br_d = st.dram_in("br", [8, 1], F32)
    w1f_d = st.dram_in("W1f", [8, D, 4096], F32)
    b1f_d = st.dram_in("b1f", [8, P, 32], F32)
    w2f_d = st.dram_in("W2f", [8, 4096, D], F32)
    b2f_d = st.dram_in("b2f", [8, D], F32)

    x2_d = st.dram_out("x2", [NB * S, D], F32)

    ld = p.ctr("ld")
    st.load_consts(ld)
    cc = make_consts_c()
    gq_it = nc.inline_tensor(cc["c_gq"], "il_c_gq")
    arr = cc["c_scol"]
    t = st.sbuf("c_scol", list(arr.shape), F32)
    it = nc.inline_tensor(arr, "il_c_scol")
    p.op("gp", lambda eng, t=t, it=it:
        eng.dma_start(out=t[:], in_=it.ap()), ld, 16)
    st.cst["c_scol"] = t

    x2a = st.sbuf("x2a", [P, NT, D], F32)
    xn1T = st.sbuf("xn1T", [P, KC, S], BF16)
    retTn = st.sbuf("retTn", [P, KC, S], BF16)
    w1rb = st.sbuf("w1rb", [P, KC, D], BF16)
    w2rb = st.sbuf("w2rb", [P, KC, D], BF16)
    hTg = st.sbuf("hTg", [P, KC, 512], BF16)
    qsT = st.sbuf("qsT", [P, S], BF16)
    ksT = st.sbuf("ksT", [P, S], BF16)
    vtm = hTg[:, 0:4, :].rearrange("p a (b c) -> p (a b) c", c=HD)
    sbt = [st.sbuf(f"sbt{i}", [P, 512], BF16) for i in range(2)]
    yg = st.sbuf("yg", [P, D], BF16)
    xng = [st.sbuf(f"xng{i}", [P, D], BF16) for i in range(2)]
    lnscr = xng[0][:]
    if os.environ.get("KRA"):
        rscr = st.sbuf("rscrX", [P, 512], BF16)[:]
    else:
        rscr = xng[1][:, 0:512]
    gqb = st.sbuf("gqb", [P, 512], BF16)
    gqrow = st.sbuf("gqrow", [1, 512], BF16)
    rtb = [st.sbuf(f"rtb{i}", [P, HD], BF16) for i in range(2)]
    silb = [st.sbuf(f"silbC{i}", [P, 512], BF16) for i in range(2)]
    wqhb = st.sbuf("wqhb", [P, NHH, HD], BF16)
    wkhb = st.sbuf("wkhb", [P, NHH, HD], BF16)
    wvhb = st.sbuf("wvhb", [P, NHH, HD], BF16)
    wrrb = st.sbuf("wrrb", [P, KC, 8], BF16)
    brcol = st.sbuf("brcolC", [8, 1], F32)
    gnrow = st.sbuf("gnrow", [1, D], BF16)
    gnwb = st.sbuf("gnwb", [P, D], BF16)
    gnbb = st.sbuf("gnbb", [P, D], BF16)
    b2rb = gnrow[:]
    b1c = [st.sbuf(f"b1c{i}", [P, KC], F32) for i in range(2)]
    lgtw = [st.sbuf(f"lgtw{i}", [8, 512], BF16) for i in range(2)]
    lgn = st.sbuf("lgnC", [P, NT, 8], F32)
    gates = st.sbuf("gates", [P, NT, 8], F32)
    prb2 = st.sbuf("prb2", [P, 16], F32)
    pacc = st.sbuf("pacc", [P, 8], F32)
    ptmp = st.sbuf("ptmp", [P, 8], F32)
    ptm2 = st.sbuf("ptm2", [P, 8], F32)
    pmaxc = st.sbuf("pmaxc", [P, 1], F32)
    pnegc = st.sbuf("pnegc", [P, 1], F32)
    psumc = st.sbuf("psumc", [P, 1], F32)
    sum16 = st.sbuf("sum16C", [P, NT], F32)
    ssq16 = st.sbuf("ssq16C", [P, NT], F32)
    mu16 = st.sbuf("mu16C", [P, NT], F32)
    var16 = st.sbuf("var16C", [P, NT], F32)
    rstd16 = st.sbuf("rstd16C", [P, NT], F32)
    nb16 = st.sbuf("nb16C", [P, NT], F32)
    rs1 = st.sbuf("rs1", [P, 4], F32)
    rs2 = st.sbuf("rs2", [P, 4], F32)
    rmu = st.sbuf("rmu", [P, 4], F32)
    rvar = st.sbuf("rvar", [P, 4], F32)
    rrstd = st.sbuf("rrstd", [P, 4], F32)
    rnb = st.sbuf("rnb", [P, 4], F32)
    posb = st.sbuf("posb", [P, 2, NB * NT], I32)

    sp = [st.psum(f"spC{i}", [P, 512]) for i in range(2)]
    rpx = [st.psum(f"rpxC{i}", [P, 512]) for i in range(4)]
    hpp = sp
    ypp = [rpx[0], rpx[1]]
    pmbC = st.psum("pmbC", [P, 32], BF16)

    C = {k: p.ctr(k) for k in (
        "x1l", "ygz", "gy", "cmb", "sq", "st", "d1", "d2", "d3", "xnc",
        "tpA", "tpB", "posl", "wqkv",
        "qm", "qe", "vm", "ve", "sm", "se", "rm", "rsq", "rst", "rd", "rda", "tpe",
        "rev", "rgn", "rtpA", "rtpB",
        "wr3", "gm", "ge", "go", "wm2", "x2e",
        "pm3", "pe3", "pt3", "pv3", "tp3",
        "gnm", "gne", "gnl", "gql", "gqm", "gqe",
        "wlA", "wlB", "b2l", "hm4", "he4", "ym4", "ye4", "xs4")}

    def W(eng_name, cname, v):
        p.wait(eng_name, C[cname], v)

    regs = {}

    def _mkregs(eng):
        r = eng.alloc_register()
        eng.reg_mov(r, 8 * CAP - 1)
        regs["yb"] = r

    p.op("gp", _mkregs)

    ld_full = ld.n
    for e in ENGINES:
        p.wait(e, ld, ld_full)

    p.op("gp", lambda eng: eng.dma_start(
        out=posb[:], in_=pos_d.rearrange("s p t -> p s t")), C["posl"], 16)
    p.op("gp", lambda eng: eng.dma_start(
        out=wqhb[:], in_=wq_d.rearrange("h d e -> d h e")), C["wqkv"], 16)
    p.op("gp", lambda eng: eng.dma_start(
        out=wkhb[:], in_=wk_d.rearrange("h d e -> d h e")), C["wqkv"], 16)
    p.op("gp", lambda eng: eng.dma_start(
        out=wvhb[:], in_=wv_d.rearrange("h d e -> d h e")), C["wqkv"], 16)
    p.op("gp", lambda eng: eng.dma_start(
        out=wrrb[:], in_=wr_d.rearrange("(kc p) e -> p kc e", p=P)), C["wqkv"], 16)
    p.op("gp", lambda eng: eng.dma_start(out=brcol[:], in_=br_d), C["wqkv"], 16)
    for which, dst, src in ((0, gnwb, gnw_d), (1, gnbb, gnb_d)):
        if which == 1:
            W("gp", "gnm", C["gnm"].n)
        p.op("gp", lambda eng, src=src: eng.dma_start(out=gnrow[:], in_=src), C["gnl"], 16)
        W("pe", "gnl", C["gnl"].n)
        for half in range(2):
            p.op("pe", lambda eng, half=half: eng.matmul(
                out=sp[0][:, 0:512], lhsT=st.cst["c_ones_row_bf"][0:1, :],
                rhs=gnrow[0:1, half * 512:(half + 1) * 512], start=True, stop=True),
                C["gnm"], 1)
            W("act", "gnm", C["gnm"].n)
            W("act", "gne", C["gne"].n)
            p.op("act", lambda eng, dst=dst, half=half: eng.activation(
                out=dst[:, half * 512:(half + 1) * 512], in_=sp[0][:, 0:512],
                func=AF.Copy), C["gne"], 1)
            W("pe", "gne", C["gne"].n)

    for batch in range(NB):
        base = batch * NT
        snap = {k: C[k].n for k in C}
        if batch > 0:
            for e in ENGINES:
                for cn in C:
                    if snap[cn] > 0:
                        p.wait(e, C[cn], snap[cn])

        W("gp", "posl", 16)
        for j in range(NT):
            p.op("gp", lambda eng, j=j, base=base: eng.dma_start(
                out=x2a[:, j, :], in_=tok_ap(x1p_d, base + j)), C["x1l"], 16)
        x1l_full = C["x1l"].n
        for j in range(NT):
            if j > 0 or batch > 0:
                W("dve", "cmb", C["cmb"].n)
            p.op("dve", lambda eng: eng.memset(yg[:], 0.0), C["ygz"], 1)
            W("gp", "ygz", C["ygz"].n)
            for sl in range(2):
                if sl == 1:
                    W("gp", "gy", C["gy"].n)
                p.op_ind("gp", lambda eng, sl=sl, j=j, base=base: eng.indirect_dma_start(
                    out=yg[:], out_offset=None,
                    in_=yfull_d[:, :],
                    in_offset=IndirectOffsetOnAxis(ap=posb[:, sl, base + j:base + j + 1], axis=0),
                    bounds_check=regs["yb"], oob_is_err=False,
                    compute_op=ALU.add), C["gy"])
            W("dve", "x1l", x1l_full)
            W("dve", "gy", C["gy"].n)
            p.op("dve", lambda eng, j=j: eng.tensor_add(
                out=x2a[:, j, :], in0=x2a[:, j, :], in1=yg[:]), C["cmb"], 1)
            W("act", "cmb", C["cmb"].n)
            if j > 0 or batch > 0:
                W("act", "st", C["st"].n)
            p.op("act", lambda eng, j=j: eng.activation(
                out=lnscr[:], in_=x2a[:, j, :], func=AF.Square), C["sq"], 1)
            W("dve", "cmb", C["cmb"].n)
            p.op("dve", lambda eng, j=j: eng.reduce_sum(
                out=sum16[:, j:j + 1], in_=x2a[:, j, :], axis=AX.X), C["st"], 1)
            W("dve", "sq", C["sq"].n)
            p.op("dve", lambda eng, j=j: eng.reduce_sum(
                out=ssq16[:, j:j + 1], in_=lnscr[:], axis=AX.X), C["st"], 1)
        W("dve", "st", C["st"].n)
        p.op("dve", lambda eng: eng.tensor_scalar_mul(
            out=mu16[:], in0=sum16[:], scalar1=1.0 / D), C["d1"], 1)
        W("dve", "d1", C["d1"].n)
        p.op("dve", lambda eng: eng.tensor_mul(
            out=nb16[:], in0=mu16[:], in1=mu16[:]), C["d1"], 1)
        p.op("dve", lambda eng: eng.tensor_scalar_mul(
            out=var16[:], in0=ssq16[:], scalar1=1.0 / D), C["d1"], 1)
        W("dve", "d1", C["d1"].n)
        p.op("dve", lambda eng:
            eng.tensor_sub(out=var16[:], in0=var16[:], in1=nb16[:]), C["d1"], 1)
        W("act", "d1", C["d1"].n)
        p.op("act", lambda eng: eng.activation(
            out=rstd16[:], in_=var16[:], func=AF.Ln, bias=1e-5), C["d2"], 1)
        W("act", "d2", C["d2"].n)
        p.op("act", lambda eng: eng.activation(
            out=rstd16[:], in_=rstd16[:], func=AF.Exp, scale=-0.5), C["d2"], 1)
        W("dve", "d2", C["d2"].n)
        p.op("dve", lambda eng: eng.scalar_tensor_tensor(
            out=nb16[:], in0=mu16[:], scalar=-1.0, in1=rstd16[:],
            op0=ALU.mult, op1=ALU.mult), C["d3"], 1)
        W("dve", "d3", C["d3"].n)
        for j in range(NT):
            b = j % 2
            tpn = "tpA" if b == 0 else "tpB"
            tpS = C[tpn]
            if j >= 2 or batch > 0:
                W("dve", tpn, tpS.n)
            p.op("dve", lambda eng, j=j, b=b: eng.tensor_scalar(
                out=xng[b][:], in0=x2a[:, j, :],
                scalar1=rstd16[:, j:j + 1], scalar2=nb16[:, j:j + 1],
                op0=ALU.mult, op1=ALU.add), C["xnc"], 1)
            W("sync", "xnc", C["xnc"].n)
            for k in range(KC):
                p.op("sync", lambda eng, j=j, b=b, k=k:
                    eng.dma_start(out=xn1T[:, k, j * P:(j + 1) * P],
                                  in_=xng[b][:, k * P:(k + 1) * P], transpose=True), tpS, 16)

        if CPH < 2:
            continue
        W("pe", "tpA", C["tpA"].n)
        W("pe", "tpB", C["tpB"].n)
        W("act", "tpB", C["tpB"].n)
        W("pe", "wqkv", 5 * 16)
        busy_sp = [0, 0]
        busy_sbt = [0, 0]
        rev_busy = None
        for h in range(min(NHH, KCHH)):
            if h > 0 or batch > 0:
                W("pe", "sm", C["sm"].n)
                W("pe", "rm", C["rm"].n)
            if C["gqm"].n > 0:
                W("gp", "gqm", C["gqm"].n)
            p.op("gp", lambda eng, h=h: eng.dma_start(
                out=gqrow[:], in_=gq_it.ap()[0:1, h * 512:(h + 1) * 512]), C["gql"], 16)
            if C["se"].n > 0:
                W("pe", "se", C["se"].n)
            W("pe", "gql", C["gql"].n)
            p.op("pe", lambda eng: eng.matmul(
                out=sp[1][:, 0:512], lhsT=st.cst["c_ones_row_bf"][0:1, :],
                rhs=gqrow[:], start=True, stop=True), C["gqm"], 1)
            W("act", "gqm", C["gqm"].n)
            if C["se"].n > 0:
                W("act", "se", C["se"].n)
            p.op("act", lambda eng: eng.activation(
                out=gqb[:], in_=sp[1][:, 0:512], func=AF.Copy), C["gqe"], 1)
            for w in range(4 if KCS >= 2 else 0):
                sl = w % 2
                if C["se"].n > 0:
                    W("pe", "se", C["se"].n)
                if w == 1:
                    W("pe", "gqe", C["gqe"].n)
                p.op("pe", lambda eng, h=h, w=w, sl=sl: eng.matmul(
                    out=sp[sl][:, 0:512], lhsT=wqhb[:, h, :],
                    rhs=xn1T[:, h, w * 512:(w + 1) * 512], start=True, stop=True),
                    C["qm"], 1)
                W("act", "qm", C["qm"].n)
                p.op("act", lambda eng, h=h, w=w, sl=sl: eng.activation(
                    out=qsT[:, w * 512:(w + 1) * 512], in_=sp[sl][:, 0:512],
                    func=AF.Copy), C["qe"], 1)
                W("pe", "qe", C["qe"].n)
                p.op("pe", lambda eng, h=h, w=w, sl=sl: eng.matmul(
                    out=sp[sl][:, 0:512], lhsT=wkhb[:, h, :],
                    rhs=xn1T[:, h, w * 512:(w + 1) * 512], start=True, stop=True),
                    C["qm"], 1)
                W("act", "qm", C["qm"].n)
                p.op("act", lambda eng, h=h, w=w, sl=sl: eng.activation(
                    out=ksT[:, w * 512:(w + 1) * 512], in_=sp[sl][:, 0:512],
                    func=AF.Copy), C["qe"], 1)
            for tm in range(NT if KCS >= 3 else 0):
                sl = tm % 2
                W("pe", "ve", max(0, C["ve"].n - 1))
                if tm < 2:
                    W("pe", "qe", C["qe"].n)
                p.op("pe", lambda eng, h=h, tm=tm, sl=sl: eng.matmul(
                    out=sp[sl][:, 0:HD], lhsT=xn1T[:, h, tm * P:(tm + 1) * P],
                    rhs=wvhb[:, h, :], start=True, stop=True), C["vm"], 1)
                W("act", "vm", C["vm"].n)
                p.op("act", lambda eng, h=h, tm=tm, sl=sl: eng.activation(
                    out=vtm[:, tm, :], in_=sp[sl][:, 0:HD], func=AF.Copy), C["ve"], 1)
            W("dve", "gqe", C["gqe"].n)
            for w in range(4 if KCS >= 4 else 0):
                if rev_busy:
                    p.wait("pe", C[rev_busy[0]], rev_busy[1])
                for tm in range(4 * w + 4):
                    sl = tm % 2
                    W("pe", "ve", C["ve"].n)
                    if busy_sp[sl] > 0:
                        W("pe", "se", busy_sp[sl])
                    p.op("pe", lambda eng, h=h, w=w, tm=tm, sl=sl: eng.matmul(
                        out=sp[sl][:, 0:512], lhsT=ksT[:, tm * P:(tm + 1) * P],
                        rhs=qsT[:, w * 512:(w + 1) * 512], start=True, stop=True),
                        C["sm"], 1)
                    busy_sp[sl] = C["se"].n + 1
                    W("dve", "sm", C["sm"].n)
                    W("dve", "rm", busy_sbt[sl])
                    dtc = 4 * w - tm
                    p.op("dve", lambda eng, h=h, sl=sl, dtc=dtc: eng.scalar_tensor_tensor(
                        out=sbt[sl][:], in0=sp[sl][:, 0:512],
                        scalar=st.cst["c_scol"][:, h * 19 + dtc + 3:h * 19 + dtc + 4],
                        in1=gqb[:],
                        op0=ALU.mult, op1=ALU.mult), C["se"], 1)
                    if tm // 4 == w:
                        sub = tm % 4
                        W("dve", "se", C["se"].n)
                        if sub > 0:
                            p.op("dve", lambda eng, sl=sl, sub=sub: eng.memset(
                                sbt[sl][:, 0:sub * P], 0.0), C["se"], 1)
                        p.op("dve", lambda eng, sl=sl, sub=sub: eng.tensor_mul(
                            out=sbt[sl][:, sub * P:(sub + 1) * P],
                            in0=sbt[sl][:, sub * P:(sub + 1) * P],
                            in1=st.cst["c_tri_keep_bf"][:]), C["se"], 1)
                    W("pe", "se", C["se"].n)
                    for sub in range(4 if KCS >= 5 else 0):
                        if 4 * w + sub < tm:
                            continue
                        p.op("pe", lambda eng, sl=sl, tm=tm, sub=sub, w=w: eng.matmul(
                            out=rpx[sub][:, 0:P],
                            lhsT=sbt[sl][:, sub * P:(sub + 1) * P],
                            rhs=vtm[:, tm, :],
                            start=(tm == 0), stop=(tm == 4 * w + sub)), C["rm"], 1)
                    busy_sbt[sl] = C["rm"].n
                    busy_sp[sl] = C["se"].n
                if KCS >= 6:
                    W("act", "rm", C["rm"].n)
                    if C["rst"].n > 0:
                        W("act", "rst", C["rst"].n)
                    for sub in range(4 if KCQ >= 1 else 0):
                        p.op("act", lambda eng, sub=sub: eng.activation(
                            out=rscr[:, sub * P:(sub + 1) * P], in_=rpx[sub][:, 0:P],
                            func=AF.Square), C["rsq"], 1)
                        p.op("act", lambda eng, sub=sub: eng.activation(
                            out=silb[0][:, sub * P:(sub + 1) * P], in_=rpx[sub][:, 0:P],
                            func=AF.Copy), C["rsq"], 1)
                    W("dve", "rm", C["rm"].n)
                    W("dve", "rd", C["rd"].n)
                    W("dve", "rsq", C["rsq"].n)
                    for sub in range(4 if KCQ >= 2 else 0):
                        p.op("dve", lambda eng, sub=sub: eng.reduce_sum(
                            out=rs1[:, sub:sub + 1], in_=silb[0][:, sub * P:(sub + 1) * P],
                            axis=AX.X), C["rst"], 1)
                    for sub in range(4 if KCQ >= 3 else 0):
                        p.op("dve", lambda eng, sub=sub: eng.reduce_sum(
                            out=rs2[:, sub:sub + 1], in_=rscr[:, sub * P:(sub + 1) * P],
                            axis=AX.X), C["rst"], 1)
                    W("dve", "rst", C["rst"].n)
                    if KCS < 7:
                        rev_busy = ("rst", C["rst"].n)
                        continue
                    p.op("dve", lambda eng: eng.tensor_scalar_mul(
                        out=rmu[:], in0=rs1[:], scalar1=1.0 / HD), C["rd"], 1)
                    p.op("dve", lambda eng: eng.tensor_scalar_mul(
                        out=rvar[:], in0=rs2[:], scalar1=1.0 / HD), C["rd"], 1)
                    W("dve", "rd", C["rd"].n)
                    p.op("dve", lambda eng: eng.tensor_mul(
                        out=rnb[:], in0=rmu[:], in1=rmu[:]), C["rd"], 1)
                    W("dve", "rd", C["rd"].n)
                    p.op("dve", lambda eng: eng.tensor_sub(
                        out=rvar[:], in0=rvar[:], in1=rnb[:]), C["rd"], 1)
                    W("act", "rd", C["rd"].n)
                    p.op("act", lambda eng: eng.activation(
                        out=rrstd[:], in_=rvar[:], func=AF.Ln, bias=1e-5), C["rda"], 1)
                    W("act", "rda", C["rda"].n)
                    p.op("act", lambda eng: eng.activation(
                        out=rrstd[:], in_=rrstd[:], func=AF.Exp, scale=-0.5), C["rda"], 1)
                    W("dve", "rda", C["rda"].n)
                    p.op("dve", lambda eng: eng.scalar_tensor_tensor(
                        out=rnb[:], in0=rmu[:], scalar=-1.0, in1=rrstd[:],
                        op0=ALU.mult, op1=ALU.mult), C["rd"], 1)
                    W("dve", "rd", C["rd"].n)
                    if KCS < 8:
                        rev_busy = ("rd", C["rd"].n)
                        continue
                    for sub in range(4):
                        tn = 4 * w + sub
                        b2_ = sub % 2
                        rtpn = "rtpA" if b2_ == 0 else "rtpB"
                        rtS = C[rtpn]
                        if rtS.n > 0:
                            W("dve", rtpn, rtS.n)
                        p.op("dve", lambda eng, sub=sub, b2_=b2_: eng.tensor_scalar(
                            out=rtb[b2_][:], in0=rpx[sub][:, 0:P],
                            scalar1=rrstd[:, sub:sub + 1], scalar2=rnb[:, sub:sub + 1],
                            op0=ALU.mult, op1=ALU.add), C["rev"], 1)
                        W("dve", "rev", C["rev"].n)
                        if KCG >= 1:
                            p.op("dve", lambda eng, h=h, b2_=b2_: eng.tensor_mul(
                                out=rtb[b2_][:], in0=rtb[b2_][:],
                                in1=gnwb[:, h * HD:(h + 1) * HD]), C["rev"], 1)
                            W("dve", "rev", C["rev"].n)
                        if KCG >= 2:
                            p.op("dve", lambda eng, h=h, b2_=b2_: eng.tensor_add(
                                out=rtb[b2_][:], in0=rtb[b2_][:],
                                in1=gnbb[:, h * HD:(h + 1) * HD]), C["rgn"], 1)
                        else:
                            p.op("dve", lambda eng, b2_=b2_: eng.tensor_copy(
                                out=rtb[b2_][:], in_=rtb[b2_][:]), C["rgn"], 1)
                        if KCS >= 9:
                            W("sync", "rgn", C["rgn"].n)
                            p.op("sync", lambda eng, h=h, tn=tn, b2_=b2_:
                                eng.dma_start(out=retTn[:, h, tn * P:(tn + 1) * P],
                                              in_=rtb[b2_][:], transpose=True), rtS, 16)
                    rev_busy = ("rev", C["rev"].n)

        if CPH < 3:
            continue
        if batch > 0:
            W("gp", "hm4", snap["hm4"])
            W("gp", "ym4", snap["ym4"])
        p.op("gp", lambda eng: eng.dma_start(
            out=w1rb[:], in_=w1r_d.rearrange("(kc p) f -> p kc f", p=P)), C["wr3"], 16)
        p.op("gp", lambda eng: eng.dma_start(
            out=w2rb[:], in_=w2r_d.rearrange("(kc p) f -> p kc f", p=P)), C["wr3"], 16)
        W("pe", "rev", C["rev"].n)
        busy_lgtw = [0, 0]
        for w in range(4):
            sl2 = w % 2
            if w > 0:
                W("pe", "pe3", C["pe3"].n)
            for kcc in range(KC):
                p.op("pe", lambda eng, w=w, kcc=kcc: eng.matmul(
                    out=sp[0][0:8, 0:512], lhsT=wrrb[:, kcc, :],
                    rhs=xn1T[:, kcc, w * 512:(w + 1) * 512],
                    start=(kcc == 0), stop=(kcc == KC - 1)), C["pm3"], 1)
            W("dve", "pm3", C["pm3"].n)
            if busy_lgtw[sl2] > 0:
                W("dve", "pt3", busy_lgtw[sl2])
            p.op("dve", lambda eng, w=w, sl2=sl2: eng.tensor_scalar_add(
                out=lgtw[sl2][0:8, :], in0=sp[0][0:8, 0:512],
                scalar1=brcol[:]), C["pe3"], 1)
            W("pe", "pe3", C["pe3"].n)
            if w > 0:
                W("pe", "pv3", C["pv3"].n)
            for jq in range(4):
                j = w * 4 + jq
                p.op("pe", lambda eng, jq=jq, sl2=sl2: eng.transpose(
                    out=pmbC[0:P, jq * 8:jq * 8 + 8], in_=lgtw[sl2][0:8, jq * P:(jq + 1) * P],
                    identity=st.cst["c_ident_bf"][0:8, 0:8]), C["pt3"], 1)
                W("dve", "pt3", C["pt3"].n)
                p.op("dve", lambda eng, j=j, jq=jq: eng.tensor_copy(
                    out=lgn[:, j, :], in_=pmbC[0:P, jq * 8:jq * 8 + 8]), C["pv3"], 1)
            busy_lgtw[sl2] = C["pt3"].n
        for j in range(NT):
            W("dve", "pv3", C["pv3"].n)
            W("dve", "tp3", C["tp3"].n)
            p.op("dve", lambda eng, j=j: eng.reduce_max(
                out=pmaxc[:], in_=lgn[:, j, :], axis=AX.X), C["tp3"], 1)
            W("dve", "tp3", C["tp3"].n)
            p.op("dve", lambda eng, j=j: eng.tensor_single_scalar(
                out=ptmp[:], in_=lgn[:, j, :], scalar=pmaxc[:],
                op=ALU.is_ge), C["tp3"], 1)
            p.op("dve", lambda eng: eng.tensor_scalar_mul(
                out=pnegc[:], in0=pmaxc[:], scalar1=-1.0), C["tp3"], 1)
            W("act", "tp3", C["tp3"].n)
            p.op("act", lambda eng, j=j: eng.activation(
                out=pacc[:], in_=lgn[:, j, :], func=AF.Exp, bias=pnegc[:]), C["tpe"], 1)
            W("dve", "tpe", C["tpe"].n)
            p.op("dve", lambda eng: eng.reduce_sum(
                out=psumc[:], in_=pacc[:], axis=AX.X), C["tp3"], 1)
            W("dve", "tp3", C["tp3"].n)
            p.op("dve", lambda eng: eng.reciprocal(
                out=psumc[:], in_=psumc[:]), C["tp3"], 1)
            W("dve", "tp3", C["tp3"].n)
            p.op("dve", lambda eng, j=j: eng.tensor_scalar_mul(
                out=lgn[:, j, :], in0=pacc[:], scalar1=psumc[:]), C["tp3"], 1)
            W("dve", "tp3", C["tp3"].n)
            p.op("dve", lambda eng, j=j: eng.tensor_copy(
                out=prb2[:, 0:8], in_=lgn[:, j, :]), C["tp3"], 1)
            p.op("dve", lambda eng, j=j: eng.tensor_copy(
                out=prb2[:, 8:16], in_=lgn[:, j, :]), C["tp3"], 1)
            p.op("dve", lambda eng, j=j: eng.tensor_copy(
                out=pacc[:], in_=lgn[:, j, :]), C["tp3"], 1)
            W("dve", "tp3", C["tp3"].n)
            for s in range(1, 8):
                p.op("dve", lambda eng, j=j, s=s: eng.tensor_tensor(
                    out=ptm2[:], in0=prb2[:, s:s + 8], in1=lgn[:, j, :],
                    op=ALU.is_ge), C["tp3"], 1)
                W("dve", "tp3", C["tp3"].n)
                p.op("dve", lambda eng, s=s: eng.tensor_mul(
                    out=ptm2[:], in0=ptm2[:], in1=prb2[:, s:s + 8]), C["tp3"], 1)
                W("dve", "tp3", C["tp3"].n)
                p.op("dve", lambda eng: eng.tensor_add(
                    out=pacc[:], in0=pacc[:], in1=ptm2[:]), C["tp3"], 1)
                W("dve", "tp3", C["tp3"].n)
            p.op("dve", lambda eng: eng.tensor_single_scalar(
                out=pacc[:], in_=pacc[:], scalar=0.8, op=ALU.is_lt), C["tp3"], 1)
            W("dve", "tp3", C["tp3"].n)
            p.op("dve", lambda eng: eng.tensor_tensor(
                out=pacc[:], in0=pacc[:], in1=ptmp[:], op=ALU.max), C["tp3"], 1)
            W("dve", "tp3", C["tp3"].n)
            p.op("dve", lambda eng, j=j: eng.tensor_mul(
                out=gates[:, j, :], in0=lgn[:, j, :], in1=pacc[:]), C["tp3"], 1)
            W("dve", "tp3", C["tp3"].n)
        W("pe", "wr3", C["wr3"].n)
        W("pe", "pv3", C["pv3"].n)
        rtp_full = (C["rtpA"].n, C["rtpB"].n)
        for kco in range(KC):
            for w in range(4):
                sl = (kco * 4 + w) % 2
                W("pe", "ge", max(0, C["ge"].n - 1))
                for kci in range(KC):
                    p.op("pe", lambda eng, kco=kco, w=w, sl=sl, kci=kci: eng.matmul(
                        out=sp[sl][:, 0:512], lhsT=w1rb[:, kci, kco * P:(kco + 1) * P],
                        rhs=xn1T[:, kci, w * 512:(w + 1) * 512],
                        start=(kci == 0), stop=(kci == KC - 1)), C["gm"], 1)
                W("act", "gm", C["gm"].n)
                if kco == 0 and w == 0:
                    W("act", "rst", C["rst"].n)
                if C["go"].n > 0:
                    W("act", "go", max(0, C["go"].n - 1))
                p.op("act", lambda eng, sl=sl: eng.activation(
                    out=silb[sl][:], in_=sp[sl][:, 0:512], func=AF.Silu), C["ge"], 1)
                W("dve", "ge", C["ge"].n)
                W("dve", "rtpA", rtp_full[0])
                W("dve", "rtpB", rtp_full[1])
                p.op("dve", lambda eng, kco=kco, w=w, sl=sl: eng.tensor_add(
                    out=retTn[:, kco, w * 512:(w + 1) * 512], in0=silb[sl][:],
                    in1=retTn[:, kco, w * 512:(w + 1) * 512]), C["go"], 1)
        go_full = C["go"].n
        W("pe", "go", go_full)
        for half in range(2):
            for j in range(NT):
                sl = j % 2
                W("pe", "x2e", max(0, C["x2e"].n - 1))
                for kf in range(KC):
                    p.op("pe", lambda eng, half=half, j=j, sl=sl, kf=kf: eng.matmul(
                        out=sp[sl][:, 0:512], lhsT=retTn[:, kf, j * P:(j + 1) * P],
                        rhs=w2rb[:, kf, half * 512:(half + 1) * 512],
                        start=(kf == 0), stop=(kf == KC - 1)), C["wm2"], 1)
                W("dve", "wm2", C["wm2"].n)
                p.op("dve", lambda eng, half=half, j=j, sl=sl: eng.tensor_add(
                    out=x2a[:, j, half * 512:(half + 1) * 512], in0=sp[sl][:, 0:512],
                    in1=x2a[:, j, half * 512:(half + 1) * 512]), C["x2e"], 1)

        if CPH < 4:
            continue
        x2e_full = C["x2e"].n
        W("dve", "x2e", x2e_full)
        W("pe", "x2e", x2e_full)
        W("dve", "tp3", C["tp3"].n)
        busy_h = [0, 0]
        busy_y = [0, 0]
        busy_he = [0, 0]
        b2_busy = 0
        htg_busy = 0
        for e in range(8):
            if b2_busy > 0:
                W("gp", "ym4", b2_busy)
            if batch == 0 and e == 0:
                W("gp", "gnm", C["gnm"].n)
            p.op("gp", lambda eng, e=e: eng.dma_start(
                out=b2rb[:], in_=b2f_d[e:e + 1, :]), C["b2l"], 16)
            for fq in range(NFQ):
                slot = (e * NFQ + fq) % 2
                wln = "wlA" if slot == 0 else "wlB"
                wlS = C[wln]
                w1q = w1rb if slot == 0 else w2rb
                if busy_h[slot] > 0:
                    W("gp", "hm4", busy_h[slot])
                    W("gp", "ym4", busy_y[slot])
                    W("gp", "he4", busy_he[slot])
                if e == 0 and fq < 2:
                    W("gp", "go", go_full)
                    W("gp", "wm2", C["wm2"].n)
                p.op("gp", lambda eng, e=e, fq=fq, w1q=w1q: eng.dma_start(
                    out=w1q[:], in_=w1f_d[e][:, fq * FQ:(fq + 1) * FQ]
                    .rearrange("(kc p) f -> p kc f", p=P)), wlS, 16)
                p.op("gp", lambda eng, e=e, fq=fq, slot=slot: eng.dma_start(
                    out=retTn[:].rearrange("p kc (s d) -> p kc s d", s=2)[:, :, slot, :],
                    in_=w2f_d[e][fq * FQ:(fq + 1) * FQ, :]
                    .rearrange("(kc p) d -> p kc d", p=P)), wlS, 16)
                p.op("gp", lambda eng, e=e, fq=fq, slot=slot: eng.dma_start(
                    out=b1c[slot][:], in_=b1f_d[e][:, fq * KC:(fq + 1) * KC]), wlS, 16)
                W("pe", wln, wlS.n)
                if fq == 0:
                    W("pe", "b2l", C["b2l"].n)
                for g in range(4):
                    for fp in range(KC):
                        hsl = fp % 2
                        W("pe", "he4", max(0, C["he4"].n - 1))
                        for kci in range(KC):
                            p.op("pe", lambda eng, fq=fq, g=g, fp=fp, hsl=hsl, kci=kci, w1q=w1q:
                                eng.matmul(
                                    out=hpp[hsl][:, 0:512],
                                    lhsT=w1q[:, kci, fp * P:(fp + 1) * P],
                                    rhs=xn1T[:, kci, g * 512:(g + 1) * 512],
                                    start=(kci == 0), stop=(kci == KC - 1)), C["hm4"], 1)
                        W("act", "hm4", C["hm4"].n)
                        if htg_busy > 0:
                            W("act", "ym4", htg_busy)
                        p.op("act", lambda eng, fp=fp, hsl=hsl, slot=slot: eng.activation(
                            out=hTg[:, fp, :], in_=hpp[hsl][:, 0:512], func=AF.Relu,
                            bias=b1c[slot][:, fp:fp + 1]), C["he4"], 1)
                    W("pe", "he4", C["he4"].n)
                    for sub in range(4):
                        for half in range(2):
                            ysl = (sub * 2 + half) % 2
                            W("pe", "ye4", max(0, C["ye4"].n - 1))
                            if fq == 0:
                                p.op("pe", lambda eng, half=half, ysl=ysl: eng.matmul(
                                    out=ypp[ysl][:, 0:512],
                                    lhsT=st.cst["c_ones_row_bf"][0:1, :],
                                    rhs=b2rb[0:1, half * 512:(half + 1) * 512],
                                    start=True, stop=False), C["ym4"], 1)
                            for fp in range(KC):
                                p.op("pe", lambda eng, sub=sub, half=half, ysl=ysl, fp=fp, fq=fq, slot=slot:
                                    eng.matmul(
                                        out=ypp[ysl][:, 0:512],
                                        lhsT=hTg[:, fp, sub * P:(sub + 1) * P],
                                        rhs=retTn[:, fp, slot * D + half * 512:slot * D + half * 512 + 512],
                                        start=(fq > 0 and fp == 0), stop=(fp == KC - 1)),
                                    C["ym4"], 1)
                            j = g * 4 + sub
                            W("dve", "ym4", C["ym4"].n)
                            W("dve", "ye4", C["ye4"].n)
                            p.op("dve", lambda eng, j=j, half=half, ysl=ysl, e=e:
                                eng.scalar_tensor_tensor(
                                    out=x2a[:, j, half * 512:(half + 1) * 512],
                                    in0=ypp[ysl][:, 0:512],
                                    scalar=gates[:, j, e:e + 1],
                                    in1=x2a[:, j, half * 512:(half + 1) * 512],
                                    op0=ALU.mult, op1=ALU.add), C["ye4"], 1)
                    htg_busy = C["ym4"].n
                busy_h[slot] = C["hm4"].n
                busy_y[slot] = C["ym4"].n
                busy_he[slot] = C["he4"].n
            b2_busy = C["ym4"].n

        W("sync", "ye4", C["ye4"].n)
        for j in range(NT):
            p.op("sync", lambda eng, j=j, base=base: eng.dma_start(
                out=tok_ap(x2_d, base + j), in_=x2a[:, j, :]), C["xs4"], 16)

    p.wait("sync", C["xs4"], C["xs4"].n)
    return st


_NC_CACHE = {}


def _get_nc(name, builder):
    if name not in _NC_CACHE:
        _NC_CACHE[name] = builder().finish()
    return _NC_CACHE[name]


def kernel(**inputs):
    h = np.ascontiguousarray(np.asarray(inputs["L0_attn_h"], dtype=np.float32))
    wqvk = np.asarray(inputs["L0_attn_Wqvk"], dtype=np.float32)
    wo = np.asarray(inputs["L0_attn_Wo"], dtype=np.float32)
    rw = np.asarray(inputs["L0_ffn_router_w"], dtype=np.float32)
    rb = np.asarray(inputs["L0_ffn_router_b"], dtype=np.float32)
    moeW = np.asarray(inputs["L0_ffn_W"], dtype=np.float32)
    moeV = np.asarray(inputs["L0_ffn_V"], dtype=np.float32)
    moeO = np.asarray(inputs["L0_ffn_W_out"], dtype=np.float32)
    wq1 = np.asarray(inputs["L1_attn_Wq"], dtype=np.float32)
    wk1 = np.asarray(inputs["L1_attn_Wk"], dtype=np.float32)
    wv1 = np.asarray(inputs["L1_attn_Wv"], dtype=np.float32)
    w1r = np.asarray(inputs["L1_attn_W1"], dtype=np.float32)
    w2r = np.asarray(inputs["L1_attn_W2"], dtype=np.float32)
    gnw = np.asarray(inputs["L1_attn_gn_w"], dtype=np.float32)
    gnb = np.asarray(inputs["L1_attn_gn_b"], dtype=np.float32)
    wr1 = np.asarray(inputs["L1_ffn_Wr"], dtype=np.float32)
    br1 = np.asarray(inputs["L1_ffn_br"], dtype=np.float32)
    w1f = np.asarray(inputs["L1_ffn_W1"], dtype=np.float32)
    b1f = np.asarray(inputs["L1_ffn_b1"], dtype=np.float32)
    w2f = np.asarray(inputs["L1_ffn_W2"], dtype=np.float32)
    b2f = np.asarray(inputs["L1_ffn_b2"], dtype=np.float32)

    cores = list(range(NCORES))

    ncA = _get_nc("a", build_stage_a)
    in_a = []
    for c in cores:
        h2 = np.concatenate([h[c], h[c + 8]], axis=0)
        in_a.append({"h2": h2, "Wqvk": wqvk, "Wo": wo,
                     "router_w": rw, "router_b": rb})
    resA = run_bass_kernel_spmd(ncA, in_a, core_ids=cores)

    xn_full = np.empty((NTOK, D), dtype=BF)
    lg_full = np.empty((NTOK, 8), dtype=np.float32)
    x1p = []
    for c in cores:
        r = resA.results[c]
        xo = np.asarray(r["xn_out"])
        lo = np.asarray(r["lg_out"])
        xn_full[c * S:(c + 1) * S] = xo[0:S]
        xn_full[(c + 8) * S:(c + 9) * S] = xo[S:2 * S]
        lg_full[c * S:(c + 1) * S] = lo[0:S]
        lg_full[(c + 8) * S:(c + 9) * S] = lo[S:2 * S]
        x1p.append(np.asarray(r["x1p_out"]))

    lg_pt = np.ascontiguousarray(
        lg_full.reshape(NTT, P, 8).transpose(1, 0, 2))

    ncB = _get_nc("b", build_stage_b)
    in_b = []
    for c in cores:
        esel = np.zeros((1, 8), np.float32)
        esel[0, c] = 1.0
        in_b.append({"xn_full": xn_full, "lg_full": lg_pt,
                     "We": moeW[c], "Ve": moeV[c], "WOe": moeO[c],
                     "esel": esel})
    resB = run_bass_kernel_spmd(ncB, in_b, core_ids=cores)

    yfull = np.concatenate(
        [np.asarray(resB.results[c]["y_out"]) for c in cores], axis=0)
    pos = np.asarray(resB.results[0]["pos_out"])

    ncC = _get_nc("c", build_stage_c)
    b1t = np.ascontiguousarray(b1f.reshape(8, 32, P).transpose(0, 2, 1))
    in_c = []
    for c in cores:
        tiles = np.r_[c * NT:(c + 1) * NT, (c + 8) * NT:(c + 9) * NT]
        in_c.append({
            "x1p": x1p[c], "pos": np.ascontiguousarray(pos[:, :, tiles]),
            "yfull": yfull,
            "Wq": wq1, "Wk": wk1, "Wv": wv1, "W1r": w1r, "W2r": w2r,
            "gnw": gnw[None, :], "gnb": gnb[None, :],
            "Wr": wr1, "br": br1[:, None],
            "W1f": w1f, "b1f": b1t, "W2f": w2f, "b2f": b2f,
        })
    resC = run_bass_kernel_spmd(ncC, in_c, core_ids=cores)

    out = np.empty((16, S, D), dtype=np.float32)
    for c in cores:
        x2 = np.asarray(resC.results[c]["x2"])
        out[c] = x2[0:S]
        out[c + 8] = x2[S:2 * S]
    return out


if __name__ == "__main__":
    st = build_stage_a()
    nc = st.finish()
    print("stage A instructions:", nc.n_instructions())
